# version 17
# baseline (speedup 1.0000x reference)
import contextlib
import numpy as np
import ml_dtypes
import concourse.bass as bass
import concourse.mybir as mybir
from concourse.bass_utils import run_bass_kernel_spmd
from concourse.alu_op_type import AluOpType as OP

F32 = mybir.dt.float32
BF16 = mybir.dt.bfloat16
F16 = mybir.dt.float16
U8 = mybir.dt.uint8
AF = mybir.ActivationFunctionType
AX = mybir.AxisListType
bfnp = ml_dtypes.bfloat16

NC = 8
BPC = 2
T = 2048
D = 1024
DCH = 8
H = 4096
HCH = 32
N16 = 16
TBA = 512
NBA = T // TBA
TBF = 512
NBF = T // TBF
MAGIC = 1536.0

_CACHE = {}


class Prog:

    def __init__(self, nc, sems):
        self.nc = nc
        self.sems = sems
        self.cnt = {k: 0 for k in sems}
        self.ops = []

    def op(self, eng, fn, waits=(), inc=None, amt=None):
        incs = []
        if inc is not None:
            a = amt if amt is not None else (16 if inc in ("ld", "so") else 1)
            self.cnt[inc] += a
            incs.append((inc, a))
        self.ops.append((eng, [w for w in waits if w[1] > 0], fn, incs))
        return dict(self.cnt)

    def cur(self, sem):
        return self.cnt[sem]

    def replay(self, eng_name, eng):
        for e, waits, fn, incs in self.ops:
            if e != eng_name:
                continue
            for s, v in waits:
                eng.wait_ge(self.sems[s], v)
            inst = fn(eng)
            for s, a in incs:
                inst.then_inc(self.sems[s], a)


def build_program(debug=False):
    nc = bass.Bass("TRN2", target_bir_lowering=False,
                   detect_race_conditions=False, debug=False)

    dram = {}

    def din(name, shape, dtype=F32):
        dram[name] = nc.dram_tensor(name, shape, dtype, kind="ExternalInput").ap()
        return dram[name]

    xT = din("xT", [BPC, D, T])
    winT = din("winT", [D, 2 * D])
    wxpT = din("wxpT", [D, 96])
    wdtT = din("wdtT", [64, D])
    woutT = din("woutT", [D, D])
    wskT = din("wskT", [D, D])
    w1T = din("w1T", [D, H])
    w2T = din("w2T", [H, D])
    b_in = din("b_in", [128, 16])
    cw1 = din("cw1", [128, DCH * 4])
    cb1 = din("cb1", [128, DCH])
    bxp = din("bxp", [96, 1])
    bdt = din("bdt", [128, DCH])
    alog = din("alog", [128, N16])
    dpar = din("dpar", [128, DCH])
    b_out = din("b_out", [128, DCH])
    b_sk = din("b_sk", [128, DCH])
    cw2 = din("cw2", [128, HCH * 3])
    cb2 = din("cb2", [128, HCH])
    identh = din("ident", [128, 128])

    out_h = nc.dram_tensor("out", [BPC, D, T], F32, kind="ExternalOutput").ap()
    x1_h = nc.dram_tensor("x1sc", [BPC, D, T], BF16, kind="Internal").ap()
    w1q_h = nc.dram_tensor("w1qsc", [HCH, D, 128], BF16, kind="Internal").ap()
    w2q_h = nc.dram_tensor("w2qsc", [H, D], BF16, kind="Internal").ap()
    dbg_h = {}
    if debug:
        dbg_h["d_xc"] = nc.dram_tensor("d_xc", [D, T], BF16,
                                       kind="ExternalOutput").ap()
        dbg_h["d_delta"] = nc.dram_tensor("d_delta", [D, T], F32,
                                          kind="ExternalOutput").ap()
        dbg_h["d_y"] = nc.dram_tensor("d_y", [D, T], F32,
                                      kind="ExternalOutput").ap()
        dbg_h["d_x1"] = nc.dram_tensor("d_x1", [D, T], BF16,
                                       kind="ExternalOutput").ap()
        dbg_h["d_ssm"] = nc.dram_tensor("d_ssm", [96, T], F32,
                                        kind="ExternalOutput").ap()
        dbg_h["d_res"] = nc.dram_tensor("d_res", [D, T], F32,
                                        kind="ExternalOutput").ap()
        dbg_h["d_xq"] = nc.dram_tensor("d_xq", [D, 520], BF16,
                                       kind="ExternalOutput").ap()
        dbg_h["d_s1"] = nc.dram_tensor("d_s1", [128, 520], F32,
                                       kind="ExternalOutput").ap()
        dbg_h["d_e1"] = nc.dram_tensor("d_e1", [128, 520], F32,
                                       kind="ExternalOutput").ap()
        dbg_h["d_hs"] = nc.dram_tensor("d_hs", [H, 520], BF16,
                                       kind="ExternalOutput").ap()
        dbg_h["d_hsil"] = nc.dram_tensor("d_hsil", [H, TBF], BF16,
                                         kind="ExternalOutput").ap()
        dbg_h["d_btmf"] = nc.dram_tensor("d_btmf", [128, 2, 520], BF16,
                                         kind="ExternalOutput").ap()
        dbg_h["d_x1s1"] = nc.dram_tensor("d_x1s1", [128, 8, 520], BF16,
                                         kind="ExternalOutput").ap()
        dbg_h["d_tmax1"] = nc.dram_tensor("d_tmax1", [128, 520], F32,
                                          kind="ExternalOutput").ap()
        dbg_h["d_s1b1"] = nc.dram_tensor("d_s1b1", [128, 520], F32,
                                         kind="ExternalOutput").ap()
        dbg_h["d_xq1"] = nc.dram_tensor("d_xq1", [D, 520], BF16,
                                        kind="ExternalOutput").ap()
        dbg_h["d_hst"] = nc.dram_tensor("d_hst", [128, HCH * 3], BF16,
                                        kind="ExternalOutput").ap()
        dbg_h["d_r2"] = nc.dram_tensor("d_r2", [D, TBF], F32,
                                       kind="ExternalOutput").ap()

    ctx = contextlib.ExitStack()

    def sbt(name, shape, dtype):
        return ctx.enter_context(nc.sbuf_tensor(name, shape, dtype))

    w_in = sbt("w_in", [128, DCH, 2 * D], BF16)
    w_out = sbt("w_out", [128, DCH, D], BF16)
    w_sk = sbt("w_sk", [128, DCH, D], BF16)
    w_xp = sbt("w_xp", [128, DCH, 96], BF16)
    w_dt = sbt("w_dt", [64, D], BF16)

    p_bin = sbt("p_bin", [128, 16], F32)
    p_cw1 = sbt("p_cw1", [128, DCH * 4], F32)
    p_cb1 = sbt("p_cb1", [128, DCH], F32)
    p_bxp = sbt("p_bxp", [96, 1], F32)
    p_bdt = sbt("p_bdt", [128, DCH], F32)
    p_A = sbt("p_A", [128, N16], F32)
    p_D = sbt("p_D", [128, DCH], F32)
    p_bos = sbt("p_bos", [128, DCH], F32)
    p_bsk = sbt("p_bsk", [128, DCH], F32)
    p_cw2 = sbt("p_cw2", [128, HCH * 3], F32)
    p_cb2 = sbt("p_cb2", [128, HCH], F32)
    ones_bf = sbt("ones_bf", [128, 1], BF16)
    xtail = sbt("xtail", [128, DCH * 3], BF16)
    ident128 = sbt("ident128", [128, 128], F32)
    identbf = sbt("identbf", [128, 128], BF16)
    p_eps = sbt("p_eps", [128, 1], F32)
    nc.const_aps.register(F32, 1e-5, p_eps[:])
    p_half = sbt("p_half", [128, 2], F32)
    nc.const_aps.register(F32, 0.5, p_half[:, 0:1])
    nc.const_aps.register(F32, -0.5, p_half[:, 1:2])
    swv = sbt("swv", [1, 8], F32)
    swrep = sbt("swrep", [128, 4], F32)
    carry = sbt("carry", [128, N16 * DCH], F32)
    ones128 = sbt("ones128", [128, 128], BF16)
    ssm32 = sbt("ssm32", [96, TBA], F32)
    bpcp = sbt("bpcp", [32, TBA], BF16)
    dt_bf = sbt("dt_bf", [64, TBA], BF16)

    AB = 116160
    arena = sbt("arena", [128, AB], U8)

    def av(off, shape, dtype):
        n = int(np.prod(shape)) * mybir.dt.size(dtype)
        assert off % 4 == 0 and off + n <= AB, f"arena {off}+{n}"
        v = arena[:, off:off + n].bitcast(dtype)
        if len(shape) > 1:
            lets = "abcd"[: len(shape)]
            v = v.rearrange(f"p ({' '.join(lets)}) -> p {' '.join(lets)}",
                            **{l: s for l, s in zip(lets, shape)})
        return v

    x32 = av(0, [DCH, TBA], F32)
    x_bf = av(16384, [DCH, TBA], BF16)
    XIW = 520
    x_in = av(24576, [DCH, XIW], BF16)
    silres = av(33024, [DCH, TBA], BF16)
    xc_bf = av(41216, [DCH, TBA], BF16)
    delta = av(49408, [DCH, TBA], F32)
    dA = av(65792, [2, DCH, TBA], BF16)
    y3_bf = av(65792, [DCH, TBA], BF16)
    dBu2 = [av(82176, [DCH, TBA], BF16),
            av(106752, [DCH, TBA], BF16)]
    tmpn = av(98560, [DCH, TBA], BF16)
    x1blk = av(82176, [DCH, TBA], BF16)
    ys = av(90368, [DCH, TBA], BF16)
    y_acc = av(98560, [DCH, TBA], F32)
    ln_t = av(98560, [DCH, TBA], F32)
    scr32 = av(82176, [2, TBA], F32)
    conv_acc = av(90368, [2, TBA], BF16)
    fca = av(99200, [2, TBF], BF16)
    sq_bf = av(24576, [DCH, TBA], BF16)

    X1W = 520
    x1sl = [av(0, [DCH, X1W], BF16),
            av(8320, [DCH, X1W], BF16)]
    btm_f = av(33280, [2, X1W], BF16)
    x2o = av(58240, [DCH, TBF], F32)
    Xq = av(16640, [DCH, X1W], BF16)
    HSW = 520
    hs = av(24960, [HCH, HSW], BF16)
    hsil = av(58240, [HCH, TBF], BF16)
    w1s = av(91008, [2, DCH, 128], BF16)
    w2s = av(95104, [2, DCH, 128], BF16)
    w2c = av(91008, [2, 16, 128], BF16)
    r2 = av(99200, [DCH, TBF], F32)
    r2f = av(99200, [DCH * TBF], F32)
    tqg = av(99200, [4, X1W], F32)
    btm = av(107520, [2, X1W], BF16)
    qz16 = av(107520, [4, X1W], F16)
    qi16 = av(111680, [4, X1W], mybir.dt.int16)
    qscr = av(16640, [3, 512], F32)

    q32bf = av(33280, [520], BF16)
    wqs32 = av(0, [3, 4096], F32)
    wqbf = av(49152, [2, 4096], BF16)
    wq16b = av(65536, [2, 4096], BF16)
    wqprt = av(81920, [40], F32)
    wq2 = av(82176, [40], F32)
    wq2bf = av(82432, [40], BF16)
    sfold = av(82688, [96], F32)
    lnm = av(49408, [TBA], F32)
    lnv = av(51456, [TBA], F32)
    lnr = av(53504, [TBA], F32)
    lnmr = av(55552, [TBA], F32)
    l2m = av(16640, [TBF], F32)
    l2v = av(18688, [TBF], F32)
    l2r = av(20736, [TBF], F32)
    l2mr = av(22784, [TBF], F32)

    BCB = 12544
    bca = sbt("bca", [128, BCB], U8)

    def bv(off, shape, dtype):
        n = int(np.prod(shape)) * mybir.dt.size(dtype)
        assert off + n <= BCB
        v = bca[:, off:off + n].bitcast(dtype)
        if len(shape) > 1:
            lets = "abcd"[: len(shape)]
            v = v.rearrange(f"p ({' '.join(lets)}) -> p {' '.join(lets)}",
                            **{l: s for l, s in zip(lets, shape)})
        return v

    Bp_b = bv(0, [4, TBA], BF16)
    Cp_b = bv(4160, [4, TBA], BF16)
    s1_b = bv(8320, [X1W], F32)
    e1_b = bv(10400, [X1W], F32)
    s2_b = bv(0, [TBF], F32)
    e2_b = bv(2080, [TBF], F32)
    hstash = bv(12480, [HCH], BF16)

    psb = [ctx.enter_context(nc.psum_tensor(f"psb{i}", [128, 512], F32))
           for i in range(8)]
    ps_la = psb[6]

    sems = {k: ctx.enter_context(nc.semaphore(k))
            for k in ("ld", "so", "sv", "ss", "st", "sg")}
    P = Prog(nc, sems)
    ev = {"blk_free": {"sv": 0, "so": 0}, "la_prev": 0}

    def dma(dst, src):
        return lambda e: e.dma_start(out=dst, in_=src)

    def bcast(dst, src_row, width):
        src = src_row.unsqueeze(1).broadcast_to([1, 128, width])
        return lambda e: e.dma_start(out=dst, in_=src)

    def brd(vec, n, w):
        return vec[:, 0:w].unsqueeze(1).broadcast_to([128, n, w])

    w = P.op("sync", dma(ident128[:], identh[:]), inc="ld")
    P.op("vector", lambda e: e.tensor_copy(identbf[:], ident128[:]),
         waits=[("ld", w["ld"])], inc="sv")
    for dst, src in [(p_bin, b_in), (p_cw1, cw1), (p_cb1, cb1), (p_bxp, bxp),
                     (p_bdt, bdt), (p_A, alog), (p_D, dpar), (p_bos, b_out),
                     (p_bsk, b_sk), (p_cw2, cw2), (p_cb2, cb2)]:
        P.op("sync", dma(dst[:], src[:]), inc="ld")
    ld_small = P.cur("ld")

    def w_cast(dst, src_c, nchunks, width):
        cev = {}
        for c in range(nchunks):
            stg = wqs32[:, c % 2, 0:width]
            wts = ([("sv", P.cur("sv"))] if c < 2
                   else [("sv", cev[c - 2])])
            w = P.op("sync", (lambda s_, cc: dma(s_, src_c[cc]))(stg, c),
                     waits=wts, inc="ld")
            v = P.op("vector",
                     (lambda d_, wdt: lambda e: e.tensor_copy(d_, wdt))
                     (dst[:, c, :], stg),
                     waits=[("ld", w["ld"])], inc="sv")
            cev[c] = v["sv"]

    w_cast(w_in, winT.rearrange("(c p) o -> c p o", p=128), DCH, 2 * D)
    w_cast(w_out, woutT.rearrange("(c p) o -> c p o", p=128), DCH, D)
    w_cast(w_sk, wskT.rearrange("(c p) o -> c p o", p=128), DCH, D)
    w_cast(w_xp, wxpT.rearrange("(c p) o -> c p o", p=128), DCH, 96)
    w = P.op("sync", dma(wqs32[0:64, 0, 0:D], wdtT[:]),
             waits=[("sv", P.cur("sv"))], inc="ld")
    P.op("vector", lambda e: e.tensor_copy(w_dt[:], wqs32[0:64, 0, 0:D]),
         waits=[("ld", w["ld"])], inc="sv")

    P.op("vector", lambda e: e.memset(ones_bf[:], 1.0), waits=[("ld", ld_small)])
    P.op("vector", lambda e: e.memset(ones128[:], 1.0))
    P.op("vector", lambda e: e.memset(p_eps[:], 1e-5))
    P.op("vector", lambda e: e.memset(p_half[:, 0:1], 0.5))
    P.op("vector", lambda e: e.memset(p_half[:, 1:2], -0.5))
    P.op("vector", lambda e: e.memset(sfold[:, 64:96], 0.0))
    P.op("vector", lambda e: e.memset(qscr[:, 2, 0:128], 0.0))
    P.op("vector", lambda e: e.tensor_tensor(p_bos[:], p_bos[:], p_bsk[:],
                                             OP.add))
    P.op("scalar", lambda e: e.activation(p_A[:], p_A[:], AF.Exp),
         waits=[("ld", ld_small)], inc="ss")
    P.op("vector", lambda e: e.tensor_scalar(p_A[:], p_A[:], -1.0, None,
                                             OP.mult),
         waits=[("ss", P.cur("ss"))], inc="sv")

    w1rows = w1T.rearrange("(c p) h -> c p h", p=128)
    w2rows = w2T.rearrange("(c p) dd -> c p dd", p=128)

    def qpass1(rows, nchunks, width, prt0):
        rev = {}
        for c in range(nchunks):
            stg = wqs32[:, c % 2, 0:width]
            wts = ([("sv", P.cur("sv"))] if c < 2
                   else [("sv", rev[c - 2])])
            w = P.op("sync", (lambda s_, cc: dma(s_, rows[cc]))(stg, c),
                     waits=wts, inc="ld")
            P.op("vector", (lambda s_, cc: lambda e: e.tensor_reduce(
                wqprt[:, prt0 + cc:prt0 + cc + 1], s_, AX.X,
                OP.add, apply_absolute_value=True))(stg, c),
                waits=[("ld", w["ld"])])
            v = P.op("vector",
                     lambda e: e.tensor_copy(sfold[:, 64:80],
                                             sfold[:, 80:96]), inc="sv")
            rev[c] = v["sv"]

    qpass1(w1rows, DCH, H, 0)
    qpass1(w2rows, HCH, D, 8)
    w = P.op("sync", dma(wq2[:, 0:40], wqprt[:, 0:40]),
             waits=[("sv", P.cur("sv"))], inc="ld")
    v = P.op("vector", lambda e: e.tensor_copy(wq2bf[:, 0:40], wq2[:, 0:40]),
             waits=[("ld", w["ld"])], inc="sv")
    t = P.op("tensor", lambda e: e.matmul(psb[4][0:1, 0:40], ones_bf[:],
                                          wq2bf[:, 0:40], start=True,
                                          stop=True),
             waits=[("sv", v["sv"])], inc="st")

    def sspacer(e):
        return e.tensor_copy(sfold[:, 64:80], sfold[:, 80:96])

    f = sfold[0:1, 0:64]
    P.op("vector", lambda e: e.tensor_copy(f[:, 16:56], psb[4][0:1, 0:40]),
         waits=[("st", t["st"])])
    P.op("vector", sspacer)
    P.op("vector", lambda e: e.tensor_tensor(f[:, 0:4], f[:, 16:20],
                                             f[:, 20:24], OP.add))
    P.op("vector", sspacer)
    P.op("vector", lambda e: e.tensor_tensor(f[:, 4:6], f[:, 0:2], f[:, 2:4],
                                             OP.add))
    P.op("vector", sspacer)
    P.op("vector", lambda e: e.tensor_tensor(swv[:, 0:1], f[:, 4:5],
                                             f[:, 5:6], OP.add))
    P.op("vector", sspacer)
    P.op("vector", lambda e: e.tensor_tensor(f[:, 8:24], f[:, 24:40],
                                             f[:, 40:56], OP.add))
    P.op("vector", sspacer)
    P.op("vector", lambda e: e.tensor_tensor(f[:, 24:32], f[:, 8:16],
                                             f[:, 16:24], OP.add))
    P.op("vector", sspacer)
    P.op("vector", lambda e: e.tensor_tensor(f[:, 0:4], f[:, 24:28],
                                             f[:, 28:32], OP.add))
    P.op("vector", sspacer)
    P.op("vector", lambda e: e.tensor_tensor(f[:, 4:6], f[:, 0:2], f[:, 2:4],
                                             OP.add))
    P.op("vector", sspacer)
    P.op("vector", lambda e: e.tensor_tensor(swv[:, 1:2], f[:, 4:5],
                                             f[:, 5:6], OP.add))
    P.op("vector", sspacer)
    P.op("vector", lambda e: e.tensor_scalar(swv[:, 0:2], swv[:, 0:2],
                                             1.0 / (D * H), 1e-5, OP.mult,
                                             OP.max))
    P.op("vector", sspacer)
    P.op("vector", lambda e: e.reciprocal(swv[:, 2:4], swv[:, 0:2]))
    P.op("vector", sspacer)
    sw = P.op("vector", lambda e: e.tensor_scalar(swv[:, 4:6], swv[:, 0:2],
                                                  1.0 / 254, None, OP.mult),
              inc="sv")
    bw = P.op("sync", bcast(swrep[:, 0:4], swv[0:1, 2:6], 4),
              waits=[("sv", sw["sv"])], inc="ld")
    sw_ld = bw["ld"]

    def qpass2(rows, dst_fn, nchunks, width, sw_col):
        sig_ev = {}
        st_ev = {}
        for c in range(nchunks):
            stg = wqs32[:, c % 3, 0:width]
            bfb = wqbf[:, c % 2, 0:width]
            s16 = wq16b[:, c % 2, 0:width]
            wts = [("ld", sw_ld)]
            if c < 3:
                wts += [("sv", P.cur("sv")), ("so", P.cur("so")),
                        ("ss", P.cur("ss"))]
            else:
                wts.append(("ss", sig_ev[c - 3]))
            w = P.op("sync", (lambda s_, cc: dma(s_, rows[cc]))(stg, c),
                     waits=wts, inc="ld")
            v = P.op("vector", (lambda s_: lambda e: e.tensor_scalar(
                s_, s_, swrep[:, sw_col:sw_col + 1], None, OP.mult))(stg),
                waits=[("ld", w["ld"])], inc="sv")
            swts = [("sv", v["sv"])]
            swts.append(("so", st_ev[c - 2]) if c >= 2
                        else ("so", P.cur("so")))
            s = P.op("scalar", (lambda b_, s_: lambda e: e.sign(
                b_, s_, bias=-0.5))(bfb, stg), waits=swts)
            s = P.op("scalar", (lambda b_, s_: lambda e: e.sign(
                b_, s_, bias=0.5))(s16, stg), inc="ss")
            sig_ev[c] = s["ss"]
            v = P.op("vector", (lambda b_, s_: lambda e: e.tensor_tensor(
                b_, b_, s_, OP.add))(bfb, s16),
                waits=[("ss", s["ss"])], inc="sv")
            st = P.op("gpsimd", (lambda b_, cc: dma(dst_fn(cc), b_))(bfb, c),
                      waits=[("sv", v["sv"])], inc="so")
            st_ev[c] = st["so"]

    qpass2(w1rows, lambda c: w1q_h[:, c * 128:(c + 1) * 128, :]
           .transpose([1, 0, 2]), DCH, H, 0)
    qpass2(w2rows, lambda c: w2q_h.rearrange("(c p) dd -> c p dd", p=128)[c],
           HCH, D, 1)
    ev["wq_done"] = P.cur("so")
    ev["blk_free"] = {"sv": P.cur("sv"), "so": P.cur("so")}


    def emit_quant_scales(tmax, W, blkw, swcol, s_dst, e_dst, fold_ev):
        npc = blkw // 128

        def spacer(e):
            return e.tensor_copy(qscr[:, 2, 0:64], qscr[:, 2, 64:128])

        for pc in range(npc):
            g = P.op("tensor", (lambda p_: lambda e: e.transpose(
                psb[4][:, p_ * 128:(p_ + 1) * 128],
                tmax[:, p_ * 128:(p_ + 1) * 128], ident128[:]))(pc),
                waits=[("sv", fold_ev)] if pc == 0 else (),
                inc="st" if (pc == npc - 1 and W == blkw) else None)
        if W > blkw:
            g = P.op("tensor", lambda e: e.transpose(
                psb[5][0:1, 0:128], tmax[:, blkw:blkw + 1], ident128[:]),
                inc="st")
        b0 = qscr[:, 0, 0:512].rearrange("p (a b) -> p a b", a=npc)
        P.op("vector", lambda e: e.tensor_copy(
            b0, psb[4][:, 0:npc * 128].rearrange("p (a b) -> p a b", a=npc)),
            waits=[("st", g["st"])])
        P.op("vector", spacer)
        tA = qscr[:, 1, 0:npc * 64].rearrange("p (a b) -> p a b", a=npc)
        tB = qscr[:, 1, 256:256 + npc * 32].rearrange("p (a b) -> p a b", a=npc)
        tC = qscr[:, 1, 384:384 + npc * 16].rearrange("p (a b) -> p a b", a=npc)
        t4 = qscr[:, 1, 448:448 + npc * 8].rearrange("p (a b) -> p a b", a=npc)
        t5 = qscr[:, 1, 480:480 + npc * 4].rearrange("p (a b) -> p a b", a=npc)
        t6 = qscr[:, 1, 496:496 + npc * 2].rearrange("p (a b) -> p a b", a=npc)
        steps = [(tA, b0, 64), (tB, tA, 32), (tC, tB, 16), (t4, tC, 8),
                 (t5, t4, 4), (t6, t5, 2)]
        for out_v, in_v, w_ in steps:
            P.op("vector", (lambda o_, i_, ww: lambda e: e.tensor_tensor(
                o_, i_[:, :, 0:ww], i_[:, :, ww:2 * ww], OP.max))
                (out_v, in_v, w_))
            P.op("vector", spacer)
        mxq = qscr[:, 1, 504:504 + npc]
        P.op("vector", lambda e: e.tensor_tensor(
            mxq.rearrange("p (a b) -> p a b", a=npc), t6[:, :, 0:1],
            t6[:, :, 1:2], OP.max))
        P.op("vector", spacer)
        mxc = qscr[:, 1, 508:508 + npc]
        sev_v = qscr[:, 0, 0:2 * npc]
        P.op("vector", lambda e: e.tensor_scalar(mxc, mxq, 1e-5, None, OP.max))
        P.op("vector", spacer)
        P.op("vector", lambda e: e.reciprocal(sev_v[:, 0:npc], mxc))
        P.op("vector", spacer)
        P.op("vector", lambda e: e.tensor_scalar(sev_v[:, 0:npc],
                                                 sev_v[:, 0:npc], 127.0, None,
                                                 OP.mult))
        P.op("vector", spacer)
        v = P.op("vector", lambda e: e.tensor_scalar(
            sev_v[:, npc:2 * npc], mxc, swrep[:, 2 + swcol:3 + swcol], None,
            OP.mult), inc="sv")
        g = P.op("tensor", lambda e: e.transpose(
            psb[5][0:2 * npc, 128:256], sev_v, ident128[:]),
            waits=[("sv", v["sv"])], inc="st")
        v = P.op("vector", lambda e: e.tensor_copy(
            qscr[0:2 * npc, 0, 16:144], psb[5][0:2 * npc, 128:256]),
            waits=[("st", g["st"])], inc="sv")
        sev = v["sv"]
        if W > blkw:
            c = qscr[0:1, 0, 160:352]
            P.op("vector", lambda e: e.tensor_copy(c[:, 0:128],
                                                   psb[5][0:1, 0:128]))
            P.op("vector", spacer)
            for (o0, o1, i0, ww) in [(128, 192, 0, 64), (0, 32, 128, 32),
                                     (32, 48, 0, 16), (48, 56, 32, 8),
                                     (56, 60, 48, 4), (60, 62, 56, 2),
                                     (62, 63, 60, 1)]:
                P.op("vector", (lambda a_, b_, c_, ww_: lambda e:
                     e.tensor_tensor(c[:, a_:b_], c[:, c_:c_ + ww_],
                                     c[:, c_ + ww_:c_ + 2 * ww_], OP.max))
                     (o0, o1, i0, ww))
                P.op("vector", spacer)
            P.op("vector", lambda e: e.tensor_scalar(c[:, 64:65], c[:, 62:63],
                                                     1e-5, None, OP.max))
            P.op("vector", spacer)
            P.op("vector", lambda e: e.reciprocal(c[:, 65:66], c[:, 64:65]))
            P.op("vector", spacer)
            P.op("vector", lambda e: e.tensor_scalar(c[:, 65:66], c[:, 65:66],
                                                     127.0, None, OP.mult))
            P.op("vector", spacer)
            v = P.op("vector", (lambda sc: lambda e: e.tensor_scalar(
                c[:, 66:67], c[:, 64:65], swv[0:1, 4 + sc:5 + sc], None,
                OP.mult))(swcol), inc="sv")
            sev = v["sv"]
        for pc in range(npc):
            P.op("sync", (lambda p_: bcast(s_dst[:, p_ * 128:(p_ + 1) * 128],
                                           qscr[p_:p_ + 1, 0, 16:144], 128))
                 (pc), waits=[("sv", sev)] if pc == 0 else (), inc="ld")
            w = P.op("scalar", (lambda p_: bcast(
                e_dst[:, p_ * 128:(p_ + 1) * 128],
                qscr[npc + p_:npc + p_ + 1, 0, 16:144], 128))(pc),
                waits=[("sv", sev)] if pc == 0 else (), inc="ld")
        if W > blkw:
            P.op("sync", bcast(s_dst[:, blkw:blkw + 1],
                               qscr[0:1, 0, 225:226], 1), inc="ld")
            w = P.op("scalar", bcast(e_dst[:, blkw:blkw + 1],
                                     qscr[0:1, 0, 226:227], 1), inc="ld")
        return w["ld"]

    def emit_attn_block(b, j):
        t0 = j * TBA
        f = ev["blk_free"]
        w = P.op("sync", dma(x32[:], xT[b].rearrange("(c p) t -> p c t", p=128)
                             [:, :, t0:t0 + TBA]),
                 waits=[("sv", f["sv"]), ("so", f["so"])], inc="ld")
        x_ld = w["ld"]
        if j == 0:
            h0 = P.op("vector", lambda e: e.memset(x_in[:, :, 1:4], 0.0),
                      inc="sv")
        else:
            h0 = P.op("vector", lambda e: e.tensor_copy(
                x_in[:, :, 1:4],
                xtail[:].rearrange("p (a b) -> p a b", a=DCH)), inc="sv")
        halo_ev = h0["sv"]
        xbf_ev = {}
        for c in range(DCH):
            v = P.op("vector", (lambda cc: lambda e: e.tensor_copy(
                x_bf[:, cc, :], x32[:, cc, :]))(c),
                waits=[("ld", x_ld)] if c == 0 else (), inc="sv")
            xbf_ev[c] = v["sv"]

        evac_ev = {}
        for o in range(16):
            bank = psb[o % 3]
            for k in range(DCH):
                wts = []
                if o == 0:
                    wts.append(("sv", xbf_ev[k]))
                elif k == 0 and o >= 3:
                    wts.append(("ss", evac_ev[o - 3]))
                g = P.op("tensor", (lambda bk, oo, kk: lambda e: e.matmul(
                    bk[:], w_in[:, kk, oo * 128:(oo + 1) * 128], x_bf[:, kk, :],
                    start=(kk == 0), stop=(kk == DCH - 1)))(bank, o, k),
                    waits=wts,
                    inc="st" if k == DCH - 1 else None)
            if o < 8:
                s = P.op("scalar", (lambda bk, oo: lambda e: e.activation(
                    x_in[:, oo, 4:516], bk[:], AF.Identity,
                    bias=p_bin[:, oo:oo + 1]))(bank, o),
                    waits=[("st", g["st"]), ("sv", halo_ev)], inc="ss")
            else:
                s = P.op("scalar", (lambda bk, oo: lambda e: e.activation(
                    silres[:, oo - 8, :], bk[:], AF.Silu,
                    bias=p_bin[:, oo:oo + 1]))(bank, o),
                    waits=[("st", g["st"])], inc="ss")
            evac_ev[o] = s["ss"]

        xc_ev = {}
        v3_ev = {}
        for c in range(DCH):
            ca = conv_acc[:, c % 2, :]
            wts = [("ss", evac_ev[c])]
            if c >= 2:
                wts.append(("ss", xc_ev[c - 2]))
            P.op("vector", (lambda cc, ca_: lambda e: e.tensor_scalar(
                ca_, x_in[:, cc, 1:513],
                p_cw1[:, cc * 4:cc * 4 + 1], None, OP.mult))(c, ca),
                waits=wts)
            P.op("vector", (lambda cc, ca_: lambda e: e.scalar_tensor_tensor(
                ca_, x_in[:, cc, 2:514],
                p_cw1[:, cc * 4 + 1:cc * 4 + 2], ca_,
                OP.mult, OP.add))(c, ca))
            P.op("vector", (lambda cc, ca_: lambda e: e.scalar_tensor_tensor(
                ca_, x_in[:, cc, 3:515],
                p_cw1[:, cc * 4 + 2:cc * 4 + 3], ca_,
                OP.mult, OP.add))(c, ca))
            v = P.op("vector", (lambda cc, ca_: lambda e:
                     e.scalar_tensor_tensor(
                         ca_, x_in[:, cc, 4:516],
                         p_cw1[:, cc * 4 + 3:cc * 4 + 4], ca_,
                         OP.mult, OP.add))(c, ca), inc="sv")
            v3_ev[c] = v["sv"]
            s = P.op("scalar", (lambda cc, ca_: lambda e: e.activation(
                xc_bf[:, cc, :], ca_, AF.Silu,
                bias=p_cb1[:, cc:cc + 1]))(c, ca),
                waits=[("sv", v["sv"])], inc="ss")
            xc_ev[c] = s["ss"]

        for k in range(DCH):
            g = P.op("tensor", (lambda kk: lambda e: e.matmul(
                psb[3][0:96, :], w_xp[:, kk, :], xc_bf[:, kk, :],
                start=(kk == 0), stop=(kk == DCH - 1)))(k),
                waits=[("ss", xc_ev[k])],
                inc="st" if k == DCH - 1 else None)
        s = P.op("scalar", lambda e: e.activation(ssm32[:], psb[3][0:96, :],
                                                  AF.Identity, bias=p_bxp[:, 0:1]),
                 waits=[("st", g["st"])], inc="ss")
        P.op("vector", lambda e: e.tensor_copy(bpcp[:], ssm32[64:96, :]),
             waits=[("ss", s["ss"]), ("ld", ev.get("bc_last", 0))])
        v = P.op("vector", lambda e: e.tensor_copy(dt_bf[:], ssm32[0:64, :]),
                 inc="sv")
        ssmc_ev = v["sv"]

        dsp_ev = {}
        for o in range(DCH):
            bank = psb[o % 3]
            wts = [("sv", ssmc_ev)]
            if o >= 3:
                wts.append(("ss", dsp_ev[o - 3]))
            g = P.op("tensor", (lambda bk, oo: lambda e: e.matmul(
                bk[:], w_dt[:, oo * 128:(oo + 1) * 128], dt_bf[:],
                start=True, stop=True))(bank, o), waits=wts, inc="st")
            P.op("scalar", (lambda bk, oo: lambda e: e.activation(
                scr32[:, 0, :], bk[:], AF.Exp, bias=p_bdt[:, oo:oo + 1]))
                (bank, o), waits=[("st", g["st"])])
            s = P.op("scalar", (lambda oo: lambda e: e.activation(
                delta[:, oo, :], scr32[:, 0, :], AF.Ln, bias=1.0))(o),
                inc="ss")
            dsp_ev[o] = s["ss"]

        P.op("vector", lambda e: e.tensor_copy(
            xtail[:].rearrange("p (a b) -> p a b", a=DCH),
            x_in[:, :, 513:516]), waits=[("ss", xc_ev[DCH - 1])])
        u_view = x_in[:, :, 4:516]
        v = P.op("vector", lambda e: e.tensor_tensor(u_view, delta[:], xc_bf[:],
                                                     OP.mult),
                 waits=[("ss", dsp_ev[DCH - 1])], inc="sv")
        u_ev = v["sv"]

        if j == 0:
            P.op("vector", lambda e: e.memset(carry[:], 0.0))
        scan_ev = {}
        gadd_ev = {}
        for n in range(N16):
            h = n % 2
            h4 = n % 4
            wts = [("sv", ssmc_ev)]
            if n >= 4:
                wts.append(("sv", scan_ev[n - 4]))
            P.op("sync", bcast(Bp_b[:, h4, :], bpcp[n:n + 1, :], TBA),
                 waits=wts, inc="ld")
            w = P.op("sync", bcast(Cp_b[:, h4, :], bpcp[16 + n:17 + n, :], TBA),
                     inc="ld")
            bc_ld = w["ld"]
            ev["bc_last"] = bc_ld
            wts = []
            if n >= 2:
                wts.append(("sv", scan_ev[n - 2]))
            s = P.op("scalar", (lambda nn, hh: lambda e: e.activation(
                dA[:, hh], delta[:], AF.Exp, scale=p_A[:, nn:nn + 1]))(n, h),
                waits=wts, inc="ss")
            dA_ev = s["ss"]
            wts = [("ld", bc_ld), ("sv", u_ev)]
            if n >= 2:
                wts.append(("sv", scan_ev[n - 2]))
            dbuf = dBu2[n % 2]
            g2 = P.op("gpsimd", (lambda db, hh: lambda e: e.tensor_tensor(
                db[:], u_view,
                Bp_b[:, hh, :].unsqueeze(1).broadcast_to([128, DCH, TBA]),
                OP.mult))(dbuf, h4), waits=wts, inc="sg")
            for c in range(DCH):
                P.op("vector", (lambda nn, hh, cc, db: lambda e:
                     e.tensor_tensor_scan(
                         ys[:, cc, :], dA[:, hh, cc, :], db[:, cc, :],
                         carry[:, nn * DCH + cc:nn * DCH + cc + 1],
                         OP.mult, OP.add))(n, h, c, dbuf),
                    waits=[("ss", dA_ev), ("sg", g2["sg"])] if c == 0 else ())
            v = P.op("vector", (lambda hh: lambda e: e.tensor_tensor(
                tmpn[:], ys[:],
                Cp_b[:, hh, :].unsqueeze(1).broadcast_to([128, DCH, TBA]),
                OP.mult))(h4),
                waits=[("st", gadd_ev[n - 1])] if n >= 1 else (), inc="sv")
            tmp_ev = v["sv"]
            for c in range(DCH):
                g = P.op("tensor", (lambda nn, cc: lambda e: e.matmul(
                    psb[cc][:], identbf[:], tmpn[:, cc, :],
                    start=(nn == 0), stop=(nn == N16 - 1),
                    skip_group_check=True))(n, c),
                    waits=[("sv", tmp_ev)] if c == 0 else (),
                    inc="st" if c == DCH - 1 else None)
            gadd_ev[n] = g["st"]
            v = P.op("vector", (lambda nn: lambda e: e.tensor_copy(
                carry[:, nn * DCH:(nn + 1) * DCH], ys[:, :, TBA - 1]))(n),
                inc="sv")
            scan_ev[n] = v["sv"]

        for c in range(DCH):
            P.op("vector", (lambda cc: lambda e: e.scalar_tensor_tensor(
                y_acc[:, cc, :], xc_bf[:, cc, :], p_D[:, cc:cc + 1],
                psb[cc][:], OP.mult, OP.add))(c),
                waits=[("st", gadd_ev[N16 - 1])] if c == 0 else ())
        if debug:
            vd = P.op("vector", lambda e: e.tensor_copy(y_acc[:, 0, 0:1],
                                                        y_acc[:, 0, 0:1]),
                      inc="sv")
            P.op("sync", dma(dbg_h["d_y"].rearrange("(c p) t -> p c t", p=128)
                             [:, :, t0:t0 + TBA], y_acc[:]),
                 waits=[("sv", vd["sv"])], inc="so")
        v = P.op("vector", lambda e: e.tensor_tensor(y3_bf[:], y_acc[:],
                                                     silres[:], OP.mult),
                 inc="sv")
        y3_ev = v["sv"]

        rres_ev = {}
        for o in range(DCH):
            bank = psb[o % 3]
            wts = [("sv", y3_ev)]
            if o >= 3:
                wts.append(("sv", rres_ev[o - 3]))
            for k in range(DCH):
                P.op("tensor", (lambda bk, oo, kk: lambda e: e.matmul(
                    bk[:], w_out[:, kk, oo * 128:(oo + 1) * 128],
                    y3_bf[:, kk, :], start=(kk == 0), stop=False))(bank, o, k),
                    waits=wts if k == 0 else ())
            for k in range(DCH):
                g = P.op("tensor", (lambda bk, oo, kk: lambda e: e.matmul(
                    bk[:], w_sk[:, kk, oo * 128:(oo + 1) * 128], x_bf[:, kk, :],
                    start=False, stop=(kk == DCH - 1)))(bank, o, k),
                    inc="st" if k == DCH - 1 else None)
            v = P.op("vector", (lambda bk, oo: lambda e:
                     e.scalar_tensor_tensor(
                         x32[:, oo, :], bk[:], p_bos[:, oo:oo + 1],
                         x32[:, oo, :], OP.add, OP.add))(bank, o),
                 waits=[("st", g["st"])], inc="sv")
            rres_ev[o] = v["sv"]
        r_ev = rres_ev[DCH - 1]
        if debug:
            P.op("sync", dma(dbg_h["d_res"].rearrange("(c p) t -> p c t", p=128)
                             [:, :, t0:t0 + TBA], x32[:]),
                 waits=[("sv", r_ev)], inc="so")
            P.op("sync", dma(dbg_h["d_xc"].rearrange("(c p) t -> p c t", p=128)
                             [:, :, t0:t0 + TBA], xc_bf[:]),
                 waits=[("sv", r_ev)], inc="so")
            P.op("sync", dma(dbg_h["d_delta"]
                             .rearrange("(c p) t -> p c t", p=128)
                             [:, :, t0:t0 + TBA], delta[:]),
                 waits=[("sv", r_ev)], inc="so")
            P.op("sync", dma(dbg_h["d_ssm"][:, t0:t0 + TBA], ssm32[:]),
                 waits=[("sv", r_ev)], inc="so")

        s = P.op("scalar", lambda e: e.activation(sq_bf[:, :, 0:TBA], x32[:],
                                                  AF.Square),
                 waits=[("sv", r_ev)], inc="ss")
        sq_ev = s["ss"]
        v = P.op("vector", lambda e: e.tensor_copy(x_bf[:], x32[:]),
                 waits=[("sv", r_ev)], inc="sv")
        rbf_ev = v["sv"]
        for k in range(DCH):
            P.op("tensor", (lambda kk: lambda e: e.matmul(
                psb[4][:], ones128[:], x_bf[:, kk, :],
                start=(kk == 0), stop=(kk == DCH - 1)))(k),
                waits=[("sv", rbf_ev)] if k == 0 else ())
        for k in range(DCH):
            g = P.op("tensor", (lambda kk: lambda e: e.matmul(
                psb[5][:], ones128[:], sq_bf[:, kk, 0:TBA],
                start=(kk == 0), stop=(kk == DCH - 1)))(k),
                waits=[("ss", sq_ev)] if k == 0 else (),
                inc="st" if k == DCH - 1 else None)
        P.op("vector", lambda e: e.tensor_scalar(lnm[:], psb[4][:],
                                                 1.0 / D, None, OP.mult),
             waits=[("st", g["st"])])
        P.op("vector", lambda e: e.tensor_scalar(lnv[:], psb[5][:],
                                                 1.0 / D, None, OP.mult))
        P.op("vector", lambda e: e.tensor_tensor(lnr[:], lnm[:], lnm[:],
                                                 OP.mult))
        v = P.op("vector", lambda e: e.tensor_tensor(
            lnv[:], lnv[:], lnr[:], OP.subtract), inc="sv")
        P.op("scalar", lambda e: e.activation(lnr[:], lnv[:], AF.Ln,
                                              bias=1e-5),
             waits=[("sv", v["sv"])])
        s = P.op("scalar", lambda e: e.activation(lnr[:], lnr[:],
                                                  AF.Exp, scale=-0.5),
                 inc="ss")
        P.op("vector", lambda e: e.tensor_tensor(
            lnmr[:], lnm[:], lnr[:], OP.mult), waits=[("ss", s["ss"])])
        P.op("vector", lambda e: e.tensor_tensor(
            ln_t[:], x32[:], brd(lnr, DCH, TBA), OP.mult))
        v = P.op("vector", lambda e: e.tensor_tensor(
            x1blk[:], ln_t[:], brd(lnmr, DCH, TBA), OP.subtract), inc="sv")
        x1_ev = v["sv"]
        st = P.op("gpsimd", dma(x1_h[b].rearrange("(c p) t -> p c t", p=128)
                                [:, :, t0:t0 + TBA], x1blk[:]),
                  waits=[("sv", x1_ev)], inc="so")
        if debug:
            P.op("sync", dma(dbg_h["d_x1"].rearrange("(c p) t -> p c t", p=128)
                             [:, :, t0:t0 + TBA], x1blk[:]),
                 waits=[("sv", x1_ev)], inc="so")
        ev["blk_free"] = {"sv": x1_ev, "so": st["so"]}
        ev[f"x1_{b}"] = st["so"]

    def emit_ffn_load(b, j):
        t0 = j * TBF
        lastb = (j == NBF - 1)
        ncols = 512 if lastb else 513
        x1s = x1sl[j % 2]
        w = P.op("sync", (lambda x_, tt, nn: dma(
            x_[:, :, 0:nn], x1_h[b].rearrange("(c p) t -> p c t", p=128)
            [:, :, tt:tt + nn]))(x1s, t0, ncols),
            waits=[("so", ev[f"x1_{b}"]),
                   ("sv", ev.get(f"x1f_free{j % 2}", 0))], inc="ld")
        ev["x1f_ld"] = w["ld"]

    def emit_ffn_trees(b, j):
        lastb = (j == NBF - 1)
        W = 513
        x1s = x1sl[j % 2]
        x1_ld = ev["x1f_ld"]
        if j > 0:
            sv0 = P.op("vector", lambda e: e.tensor_copy(
                hstash[:], hs[:, :, 512:513].rearrange("p a b -> p (a b)")),
                inc="sv")
            if debug and b == 0 and j == 1:
                P.op("sync", dma(dbg_h["d_hst"][:, 0:HCH], hstash[:]),
                     waits=[("sv", sv0["sv"])], inc="so")
        if lastb:
            P.op("vector", (lambda x_: lambda e: e.memset(
                x_[:, :, 512:513], 0.0))(x1s), waits=[("ld", x1_ld)])
        tmaxb = btm_f[:, 0, 0:W]
        tminb = btm_f[:, 1, 0:W]
        P.op("vector", (lambda x_: lambda e: e.tensor_tensor(
            tmaxb, x_[:, 0, 0:W], x_[:, 1, 0:W], OP.max))(x1s),
            waits=[("ld", x1_ld)])
        P.op("vector", (lambda x_: lambda e: e.tensor_tensor(
            tminb, x_[:, 0, 0:W], x_[:, 1, 0:W], OP.min))(x1s))
        for c in range(2, DCH):
            P.op("vector", (lambda x_, cc: lambda e: e.tensor_tensor(
                tmaxb, tmaxb, x_[:, cc, 0:W], OP.max))(x1s, c))
            P.op("vector", (lambda x_, cc: lambda e: e.tensor_tensor(
                tminb, tminb, x_[:, cc, 0:W], OP.min))(x1s, c))
        P.op("vector", lambda e: e.tensor_scalar(tminb, tminb, -1.0, None,
                                                 OP.mult))
        v = P.op("vector", lambda e: e.tensor_tensor(tmaxb, tmaxb, tminb,
                                                     OP.max), inc="sv")
        ev["x1f_tr"] = v["sv"]
        if debug and b == 0 and j == 1:
            P.op("sync", dma(dbg_h["d_btmf"][:], btm_f[:]),
                 waits=[("sv", v["sv"])], inc="so")
            P.op("sync", (lambda x_: dma(dbg_h["d_x1s1"][:], x_[:]))(x1s),
                 waits=[("sv", v["sv"])], inc="so")

    def emit_ffn_block(b, j):
        t0 = j * TBF
        lastb = (j == NBF - 1)
        W = 513
        f = ev["blk_free"]
        x1s = x1sl[j % 2]

        tmax = r2f[:, 0:W]
        v = P.op("vector", lambda e: e.tensor_copy(tmax, btm_f[:, 0, 0:W]),
                 waits=[("sv", ev["x1f_tr"]), ("ld", ev.get("stats_ld", 0)),
                        ("so", f["so"])], inc="sv")
        if debug and b == 0 and j == 1:
            P.op("sync", dma(dbg_h["d_tmax1"][:, 0:W], tmax),
                 waits=[("sv", v["sv"])], inc="so")
        sb_ld = emit_quant_scales(tmax, W, 512, 0, s1_b, e1_b, v["sv"])
        if debug and b == 0 and j == 1:
            P.op("sync", dma(dbg_h["d_s1b1"][:], s1_b[:]),
                 waits=[("ld", sb_ld)], inc="so")
        ev["stats_ld"] = sb_ld
        for g4 in range(2):
            sl = slice(g4 * 4, g4 * 4 + 4)
            P.op("vector", (lambda sl_: lambda e: e.tensor_tensor(
                tqg[:, :, 0:W], x1s[:, sl_, 0:W], brd(s1_b, 4, W), OP.mult))
                (sl), waits=[("ld", sb_ld)])
            P.op("vector", lambda e: e.tensor_scalar(
                qz16[:, :, 0:W], tqg[:, :, 0:W], MAGIC, None, OP.add))
            P.op("vector", (lambda sl_: lambda e: e.tensor_scalar(
                Xq[:, sl_, 0:W], qz16[:, :, 0:W], -MAGIC, None, OP.add))(sl))
        v = P.op("vector", lambda e: e.tensor_tensor(
            Xq[:, :, 0:W], Xq[:, :, 0:W], brd(e1_b, DCH, W), OP.mult),
            inc="sv")
        xq_ev = v["sv"]
        if debug and b == 0 and j == 1:
            P.op("sync", dma(dbg_h["d_xq1"].rearrange("(c p) t -> p c t",
                                                      p=128), Xq[:]),
                 waits=[("sv", xq_ev)], inc="so")
        if debug and b == 0 and j == 0:
            P.op("sync", dma(dbg_h["d_xq"].rearrange("(c p) t -> p c t", p=128),
                             Xq[:]), waits=[("sv", xq_ev)], inc="so")
            P.op("sync", dma(dbg_h["d_s1"][:], s1_b[:]),
                 waits=[("sv", xq_ev)], inc="so")
            P.op("sync", dma(dbg_h["d_e1"][:], e1_b[:]),
                 waits=[("sv", xq_ev)], inc="so")

        if j == 0:
            hh0 = P.op("vector", lambda e: e.memset(hs[:, :, 0:1], 0.0),
                       inc="sv")
        else:
            hh0 = P.op("vector", lambda e: e.tensor_copy(
                hs[:, :, 0:1],
                hstash[:].rearrange("p (a b) -> p a b", b=1)), inc="sv")
            if debug and b == 0 and j == 1:
                hcp = P.op("vector", lambda e: e.tensor_copy(
                    hstash[:], hs[:, :, 0:1].rearrange("p a b -> p (a b)")),
                    waits=[("sv", hh0["sv"])], inc="sv")
                P.op("sync", dma(dbg_h["d_hst"][:, 2 * HCH:3 * HCH],
                                 hstash[:]),
                     waits=[("sv", hcp["sv"])], inc="so")
        hhalo_ev = hh0["sv"]
        w1mm_ev = {}
        hs_ev = {}
        v3f_ev = {}
        hsil_ev = {}

        def emit_fsilu(hh):
            s = P.op("scalar", (lambda h_: lambda e: e.activation(
                hsil[:, h_, :], fca[:, h_ % 2, :], AF.Silu,
                bias=p_cb2[:, h_:h_ + 1]))(hh),
                waits=[("sv", v3f_ev[hh])], inc="ss")
            hsil_ev[hh] = s["ss"]

        for h in range(HCH):
            wts = [("so", ev["wq_done"])]
            if h >= 2:
                wts.append(("st", w1mm_ev[h - 2]))
            else:
                wts.append(("st", ev.get("mm2_prev", 0)))
            w = P.op("sync", (lambda hh: dma(
                w1s[:, hh % 2], w1q_h[hh].rearrange("(c p) o -> p c o",
                                                    p=128)))(h),
                waits=wts, inc="ld")
            w1_ld = w["ld"]
            bank = psb[h % 3]
            wts = [("sv", xq_ev), ("ld", w1_ld)]
            if h >= 3:
                wts.append(("ss", hs_ev[h - 3]))
            for k in range(DCH):
                P.op("tensor", (lambda bk, hh, kk: lambda e: e.matmul(
                    bk[:], w1s[:, hh % 2, kk, :], Xq[:, kk, 0:512],
                    start=(kk == 0), stop=(kk == DCH - 1)))(bank, h, k),
                    waits=wts if k == 0 else ())
            for k in range(DCH):
                lwts = []
                if h == 0 and k == 0:
                    lwts = [("ss", ev["la_prev"])]
                g = P.op("tensor", (lambda hh, kk: lambda e: e.matmul(
                    ps_la[:, hh:hh + 1], w1s[:, hh % 2, kk, :],
                    Xq[:, kk, 512:513], start=(kk == 0),
                    stop=(kk == DCH - 1)))(h, k),
                    waits=lwts, inc="st" if k == DCH - 1 else None)
            w1mm_ev[h] = g["st"]
            P.op("scalar", (lambda bk, hh: lambda e: e.activation(
                hs[:, hh, 1:513], bk[:], AF.Copy))(bank, h),
                waits=[("st", w1mm_ev[h]), ("sv", hhalo_ev)])
            s = P.op("scalar", (lambda hh: lambda e: e.activation(
                hs[:, hh, 513:514], ps_la[:, hh:hh + 1], AF.Copy))(h),
                inc="ss")
            hs_ev[h] = s["ss"]
            if h >= 2:
                emit_fsilu(h - 2)
            ca = fca[:, h % 2, :]
            cwts = [("ss", hs_ev[h])]
            if h >= 2:
                cwts.append(("ss", hsil_ev[h - 2]))
            P.op("vector", (lambda hh, ca_: lambda e: e.tensor_scalar(
                ca_, hs[:, hh, 0:512],
                p_cw2[:, hh * 3:hh * 3 + 1], None, OP.mult))(h, ca),
                waits=cwts)
            P.op("vector", (lambda hh, ca_: lambda e:
                 e.scalar_tensor_tensor(
                     ca_, hs[:, hh, 1:513],
                     p_cw2[:, hh * 3 + 1:hh * 3 + 2], ca_,
                     OP.mult, OP.add))(h, ca))
            v = P.op("vector", (lambda hh, ca_: lambda e:
                 e.scalar_tensor_tensor(
                     ca_, hs[:, hh, 2:514],
                     p_cw2[:, hh * 3 + 2:hh * 3 + 3], ca_,
                     OP.mult, OP.add))(h, ca), inc="sv")
            v3f_ev[h] = v["sv"]
        emit_fsilu(HCH - 2)
        emit_fsilu(HCH - 1)
        ev["la_prev"] = hs_ev[HCH - 1]
        if j + 1 < NBF:
            emit_ffn_load(b, j + 1)
        if debug and b == 0 and j == 0:
            P.op("sync", dma(dbg_h["d_hs"].rearrange("(c p) t -> p c t", p=128),
                             hs[:]), waits=[("ss", hsil_ev[HCH - 1])],
                 inc="so")

        if debug and b == 0 and j == 0:
            P.op("sync", dma(dbg_h["d_hsil"].rearrange("(c p) t -> p c t",
                                                       p=128), hsil[:]),
                 waits=[("ss", hsil_ev[HCH - 1])], inc="so")
        W2 = TBF
        t2maxb = btm[:, 0, 0:W2]
        t2minb = btm[:, 1, 0:W2]
        t2max = r2f[:, 0:W2]
        P.op("vector", lambda e: e.tensor_tensor(t2maxb, hsil[:, 0, :],
                                                 hsil[:, 1, :], OP.max),
             waits=[("ss", hsil_ev[1])])
        P.op("vector", lambda e: e.tensor_tensor(t2minb, hsil[:, 0, :],
                                                 hsil[:, 1, :], OP.min))
        for c in range(2, HCH):
            P.op("vector", (lambda cc: lambda e: e.tensor_tensor(
                t2maxb, t2maxb, hsil[:, cc, :], OP.max))(c),
                waits=[("ss", hsil_ev[c])])
            P.op("vector", (lambda cc: lambda e: e.tensor_tensor(
                t2minb, t2minb, hsil[:, cc, :], OP.min))(c))
        P.op("vector", lambda e: e.tensor_scalar(t2minb, t2minb, -1.0, None,
                                                 OP.mult))
        P.op("vector", lambda e: e.tensor_tensor(t2maxb, t2maxb, t2minb,
                                                 OP.max),
             waits=[("ld", ev.get("stats_ld", 0))])
        v = P.op("vector", lambda e: e.tensor_copy(t2max, t2maxb), inc="sv")
        s2_ld = emit_quant_scales(t2max, W2, 512, 1, s2_b, e2_b, v["sv"])
        ev["stats_ld"] = s2_ld
        vlast = None
        for g4 in range(8):
            sl = slice(g4 * 4, g4 * 4 + 4)
            P.op("vector", (lambda sl_: lambda e: e.tensor_tensor(
                tqg[:, :, 0:W2], hsil[:, sl_, :], brd(s2_b, 4, W2), OP.mult))
                (sl), waits=[("ld", s2_ld)])
            P.op("vector", lambda e: e.tensor_scalar(
                qz16[:, :, 0:W2], tqg[:, :, 0:W2], MAGIC, None, OP.add))
            vlast = P.op("vector", (lambda sl_: lambda e: e.tensor_scalar(
                hsil[:, sl_, :], qz16[:, :, 0:W2], -MAGIC, None, OP.add))(sl),
                inc="sv" if g4 == 7 else None)
        hq_ev = vlast["sv"]

        w2v = w2q_h.rearrange("(c p) dd -> p c dd", p=128)
        mm_ev = {}
        for dch in range(DCH):
            for half in range(2):
                idx = dch * 2 + half
                wts = [("so", ev["wq_done"])]
                if idx >= 2:
                    wts.append(("st", mm_ev[idx - 2]))
                else:
                    wts.append(("st", w1mm_ev[HCH - 1]))
                w = P.op("sync", (lambda dd, hh: dma(
                    w2c[:, hh],
                    w2v[:, hh * 16:(hh + 1) * 16,
                        dd * 128:(dd + 1) * 128]))(dch, half),
                    waits=wts, inc="ld")
                mwts = [("ld", w["ld"])]
                if idx == 0:
                    mwts.append(("sv", hq_ev))
                for k in range(16):
                    hk = half * 16 + k
                    g = P.op("tensor",
                             (lambda dd, hh, kk, hkk: lambda e: e.matmul(
                                 psb[dd][:], w2c[:, hh, kk, :],
                                 hsil[:, hkk, :], start=(hkk == 0),
                                 stop=(hkk == HCH - 1)))(dch, half, k, hk),
                             waits=mwts if k == 0 else (),
                             inc="st" if k == 15 else None)
                mm_ev[idx] = g["st"]
        mm2_ev = mm_ev[2 * DCH - 1]
        ev["mm2_prev"] = mm2_ev

        if j + 1 < NBF:
            emit_ffn_trees(b, j + 1)

        vlast = None
        for dch in range(DCH):
            P.op("vector", (lambda dd: lambda e: e.tensor_tensor(
                r2[:, dd, :], psb[dd][:], e2_b[:, 0:W2],
                OP.mult))(dch),
                waits=[("st", mm_ev[dch * 2 + 1])])
            vlast = P.op("vector", (lambda dd: lambda e: e.tensor_tensor(
                r2[:, dd, :], r2[:, dd, :], x1s[:, dd, 0:512], OP.add))(dch),
                inc="sv" if dch == DCH - 1 else None)
        r2_ev = vlast["sv"]
        ev[f"x1f_free{j % 2}"] = r2_ev

        if debug and b == 0 and j == 0:
            P.op("sync", dma(dbg_h["d_r2"].rearrange("(c p) t -> p c t", p=128),
                             r2[:]), waits=[("sv", r2_ev)], inc="so")
        s = P.op("scalar", lambda e: e.activation(hsil[:, 0:8, :], r2[:],
                                                  AF.Square),
                 waits=[("sv", r2_ev)], inc="ss")
        sq2_ev = s["ss"]
        v = P.op("vector", lambda e: e.tensor_copy(hs[:, 0:8, 0:512], r2[:]),
                 waits=[("sv", r2_ev)], inc="sv")
        r2bf_ev = v["sv"]
        for k in range(DCH):
            P.op("tensor", (lambda kk: lambda e: e.matmul(
                psb[4][:], ones128[:], hs[:, kk, 0:512],
                start=(kk == 0), stop=(kk == DCH - 1)))(k),
                waits=[("sv", r2bf_ev)] if k == 0 else ())
        for k in range(DCH):
            g = P.op("tensor", (lambda kk: lambda e: e.matmul(
                psb[5][:], ones128[:], hsil[:, kk, :],
                start=(kk == 0), stop=(kk == DCH - 1)))(k),
                waits=[("ss", sq2_ev)] if k == 0 else (),
                inc="st" if k == DCH - 1 else None)
        P.op("vector", lambda e: e.tensor_scalar(l2m[:], psb[4][:],
                                                 1.0 / D, None, OP.mult),
             waits=[("st", g["st"])])
        P.op("vector", lambda e: e.tensor_scalar(l2v[:], psb[5][:],
                                                 1.0 / D, None, OP.mult))
        P.op("vector", lambda e: e.tensor_tensor(l2r[:], l2m[:], l2m[:],
                                                 OP.mult))
        v = P.op("vector", lambda e: e.tensor_tensor(
            l2v[:], l2v[:], l2r[:], OP.subtract), inc="sv")
        P.op("scalar", lambda e: e.activation(l2r[:], l2v[:], AF.Ln,
                                              bias=1e-5),
             waits=[("sv", v["sv"])])
        s = P.op("scalar", lambda e: e.activation(l2r[:], l2r[:],
                                                  AF.Exp, scale=-0.5),
                 inc="ss")
        P.op("vector", lambda e: e.tensor_tensor(
            l2mr[:], l2m[:], l2r[:], OP.mult), waits=[("ss", s["ss"])])
        P.op("vector", lambda e: e.tensor_tensor(
            x2o[:], r2[:], brd(l2r, DCH, W2), OP.mult))
        v = P.op("vector", lambda e: e.tensor_tensor(
            x2o[:], x2o[:], brd(l2mr, DCH, W2), OP.subtract), inc="sv")
        st = P.op("sync", dma(out_h[b].rearrange("(c p) t -> p c t", p=128)
                              [:, :, t0:t0 + TBF], x2o[:]),
                  waits=[("sv", v["sv"])], inc="so")
        ev["blk_free"] = {"sv": v["sv"], "so": st["so"]}

    for b in range(BPC):
        for j in range(NBA):
            emit_attn_block(b, j)
        emit_ffn_load(b, 0)
        emit_ffn_trees(b, 0)
        for j in range(NBF):
            emit_ffn_block(b, j)

    fin = P.cur("so")
    P.ops.append(("sync", [("so", fin)], lambda e: None, []))

    with nc.Block() as block:
        for name in ("sync", "vector", "scalar", "tensor", "gpsimd"):
            def make(nm):
                def body(eng):
                    P.replay(nm, eng)
                return body
            getattr(block, name)(make(name))

    ctx.close()
    return nc


def _host_layout(inputs):
    f = np.ascontiguousarray
    i = inputs
    shared = {
        "winT": f(i["L0_attn_in_proj_w"].T),
        "wxpT": f(i["L0_attn_x_proj_w"].T),
        "wdtT": f(i["L0_attn_dt_proj_w"].T),
        "woutT": f(i["L0_attn_out_proj_w"].T),
        "wskT": f(i["L0_attn_skip_proj_w"].T),
        "w1T": f(i["L0_ffn_w1"].T),
        "w2T": f(i["L0_ffn_w2"].T),
        "b_in": f(i["L0_attn_in_proj_b"].reshape(16, 128).T),
        "cw1": f(i["L0_attn_conv_w"][:, 0, :].reshape(8, 128, 4)
                 .transpose(1, 0, 2).reshape(128, 32)),
        "cb1": f(i["L0_attn_conv_b"].reshape(8, 128).T),
        "bxp": f(i["L0_attn_x_proj_b"][:, None]),
        "bdt": f(i["L0_attn_dt_proj_b"].reshape(8, 128).T),
        "alog": f(np.repeat(i["L0_attn_A_log"].reshape(1, 16), 128, 0)),
        "dpar": f(i["L0_attn_D_param"].reshape(8, 128).T),
        "b_out": f(i["L0_attn_out_proj_b"].reshape(8, 128).T),
        "b_sk": f(i["L0_attn_skip_proj_b"].reshape(8, 128).T),
        "cw2": f(i["L0_ffn_conv_w"][:, 0, :].reshape(32, 128, 3)
                 .transpose(1, 0, 2).reshape(128, 96)),
        "cb2": f(i["L0_ffn_conv_b"].reshape(32, 128).T),
        "ident": np.eye(128, dtype=np.float32),
    }
    shared = {k: np.asarray(v, np.float32) for k, v in shared.items()}
    x = i["L0_attn_x"]
    in_maps = []
    for c in range(NC):
        m = dict(shared)
        m["xT"] = np.asarray(
            f(x[c * BPC:(c + 1) * BPC].transpose(0, 2, 1)), np.float32)
        in_maps.append(m)
    return in_maps


def kernel(**inputs):
    if "nc" not in _CACHE:
        _CACHE["nc"] = build_program(debug=False)
    nc = _CACHE["nc"]
    in_maps = _host_layout(inputs)
    res = run_bass_kernel_spmd(nc, in_maps, core_ids=list(range(NC)))
    outs = []
    for c in range(NC):
        o = np.asarray(res.results[c]["out"])
        outs.append(o.transpose(0, 2, 1))
    return np.ascontiguousarray(np.concatenate(outs, axis=0))


# version 19
# speedup vs baseline: 1.0864x; 1.0864x over previous
import contextlib
import numpy as np
import ml_dtypes
import concourse.bass as bass
import concourse.mybir as mybir
from concourse.bass_utils import run_bass_kernel_spmd
from concourse.alu_op_type import AluOpType as OP

F32 = mybir.dt.float32
BF16 = mybir.dt.bfloat16
F16 = mybir.dt.float16
U8 = mybir.dt.uint8
AF = mybir.ActivationFunctionType
AX = mybir.AxisListType
bfnp = ml_dtypes.bfloat16

NC = 8
BPC = 2
T = 2048
D = 1024
DCH = 8
H = 4096
HCH = 32
N16 = 16
TBA = 512
NBA = T // TBA
TBF = 512
NBF = T // TBF
MAGIC = 1536.0

_CACHE = {}


class Prog:

    def __init__(self, nc, sems):
        self.nc = nc
        self.sems = sems
        self.cnt = {k: 0 for k in sems}
        self.ops = []

    def op(self, eng, fn, waits=(), inc=None, amt=None):
        incs = []
        if inc is not None:
            a = amt if amt is not None else (16 if inc in ("ld", "so") else 1)
            self.cnt[inc] += a
            incs.append((inc, a))
        self.ops.append((eng, [w for w in waits if w[1] > 0], fn, incs))
        return dict(self.cnt)

    def cur(self, sem):
        return self.cnt[sem]

    def replay(self, eng_name, eng):
        for e, waits, fn, incs in self.ops:
            if e != eng_name:
                continue
            for s, v in waits:
                eng.wait_ge(self.sems[s], v)
            inst = fn(eng)
            for s, a in incs:
                inst.then_inc(self.sems[s], a)


def build_program(debug=False):
    nc = bass.Bass("TRN2", target_bir_lowering=False,
                   detect_race_conditions=False, debug=False)

    dram = {}

    def din(name, shape, dtype=F32):
        dram[name] = nc.dram_tensor(name, shape, dtype, kind="ExternalInput").ap()
        return dram[name]

    xT = din("xT", [BPC, D, T])
    winT = din("winT", [D, 2 * D])
    wxpT = din("wxpT", [D, 96])
    wdtT = din("wdtT", [64, D])
    woutT = din("woutT", [D, D])
    wskT = din("wskT", [D, D])
    w1T = din("w1T", [D, H])
    w2T = din("w2T", [H, D])
    b_in = din("b_in", [128, 16])
    cw1 = din("cw1", [128, DCH * 4])
    cb1 = din("cb1", [128, DCH])
    bxp = din("bxp", [96, 1])
    bdt = din("bdt", [128, DCH])
    alog = din("alog", [128, N16])
    dpar = din("dpar", [128, DCH])
    b_out = din("b_out", [128, DCH])
    b_sk = din("b_sk", [128, DCH])
    cw2 = din("cw2", [128, HCH * 3])
    cb2 = din("cb2", [128, HCH])
    identh = din("ident", [128, 128])

    out_h = nc.dram_tensor("out", [BPC, D, T], F32, kind="ExternalOutput").ap()
    x1_h = nc.dram_tensor("x1sc", [BPC, D, T], BF16, kind="Internal").ap()
    w1q_h = nc.dram_tensor("w1qsc", [HCH, D, 128], BF16, kind="Internal").ap()
    w2q_h = nc.dram_tensor("w2qsc", [H, D], BF16, kind="Internal").ap()
    dbg_h = {}
    if debug:
        dbg_h["d_xc"] = nc.dram_tensor("d_xc", [D, T], BF16,
                                       kind="ExternalOutput").ap()
        dbg_h["d_delta"] = nc.dram_tensor("d_delta", [D, T], F32,
                                          kind="ExternalOutput").ap()
        dbg_h["d_y"] = nc.dram_tensor("d_y", [D, T], F32,
                                      kind="ExternalOutput").ap()
        dbg_h["d_x1"] = nc.dram_tensor("d_x1", [D, T], BF16,
                                       kind="ExternalOutput").ap()
        dbg_h["d_ssm"] = nc.dram_tensor("d_ssm", [96, T], F32,
                                        kind="ExternalOutput").ap()
        dbg_h["d_res"] = nc.dram_tensor("d_res", [D, T], F32,
                                        kind="ExternalOutput").ap()
        dbg_h["d_xq"] = nc.dram_tensor("d_xq", [D, 520], BF16,
                                       kind="ExternalOutput").ap()
        dbg_h["d_s1"] = nc.dram_tensor("d_s1", [128, 520], F32,
                                       kind="ExternalOutput").ap()
        dbg_h["d_e1"] = nc.dram_tensor("d_e1", [128, 520], F32,
                                       kind="ExternalOutput").ap()
        dbg_h["d_hs"] = nc.dram_tensor("d_hs", [H, 520], BF16,
                                       kind="ExternalOutput").ap()
        dbg_h["d_hsil"] = nc.dram_tensor("d_hsil", [H, TBF], BF16,
                                         kind="ExternalOutput").ap()
        dbg_h["d_btmf"] = nc.dram_tensor("d_btmf", [128, 2, 520], BF16,
                                         kind="ExternalOutput").ap()
        dbg_h["d_x1s1"] = nc.dram_tensor("d_x1s1", [128, 8, 520], BF16,
                                         kind="ExternalOutput").ap()
        dbg_h["d_tmax1"] = nc.dram_tensor("d_tmax1", [128, 520], F32,
                                          kind="ExternalOutput").ap()
        dbg_h["d_s1b1"] = nc.dram_tensor("d_s1b1", [128, 520], F32,
                                         kind="ExternalOutput").ap()
        dbg_h["d_xq1"] = nc.dram_tensor("d_xq1", [D, 520], BF16,
                                        kind="ExternalOutput").ap()
        dbg_h["d_hst"] = nc.dram_tensor("d_hst", [128, HCH * 3], BF16,
                                        kind="ExternalOutput").ap()
        dbg_h["d_r2"] = nc.dram_tensor("d_r2", [D, TBF], F32,
                                       kind="ExternalOutput").ap()

    ctx = contextlib.ExitStack()

    def sbt(name, shape, dtype):
        return ctx.enter_context(nc.sbuf_tensor(name, shape, dtype))

    w_in = sbt("w_in", [128, DCH, 2 * D], BF16)
    w_out = sbt("w_out", [128, DCH, D], BF16)
    w_sk = sbt("w_sk", [128, DCH, D], BF16)
    w_xp = sbt("w_xp", [128, DCH, 96], BF16)
    w_dt = sbt("w_dt", [64, D], BF16)

    p_bin = sbt("p_bin", [128, 16], F32)
    p_cw1 = sbt("p_cw1", [128, DCH * 4], F32)
    p_cb1 = sbt("p_cb1", [128, DCH], F32)
    p_bxp = sbt("p_bxp", [96, 1], F32)
    p_bdt = sbt("p_bdt", [128, DCH], F32)
    p_A = sbt("p_A", [128, N16], F32)
    p_D = sbt("p_D", [128, DCH], F32)
    p_bos = sbt("p_bos", [128, DCH], F32)
    p_bsk = sbt("p_bsk", [128, DCH], F32)
    p_cw2 = sbt("p_cw2", [128, HCH * 3], F32)
    p_cb2 = sbt("p_cb2", [128, HCH], F32)
    ones_bf = sbt("ones_bf", [128, 1], BF16)
    xtail = sbt("xtail", [128, DCH * 3], BF16)
    ident128 = sbt("ident128", [128, 128], F32)
    identbf = sbt("identbf", [128, 128], BF16)
    p_eps = sbt("p_eps", [128, 1], F32)
    nc.const_aps.register(F32, 1e-5, p_eps[:])
    p_half = sbt("p_half", [128, 2], F32)
    nc.const_aps.register(F32, 0.5, p_half[:, 0:1])
    nc.const_aps.register(F32, -0.5, p_half[:, 1:2])
    swv = sbt("swv", [1, 8], F32)
    swrep = sbt("swrep", [128, 4], F32)
    carry = sbt("carry", [128, N16 * DCH], F32)
    ones128 = sbt("ones128", [128, 128], BF16)
    ssm32 = sbt("ssm32", [96, TBA], F32)
    bpcp = sbt("bpcp", [32, TBA], BF16)
    dt_bf = sbt("dt_bf", [64, TBA], BF16)

    AB = 116160
    arena = sbt("arena", [128, AB], U8)

    def av(off, shape, dtype):
        n = int(np.prod(shape)) * mybir.dt.size(dtype)
        assert off % 4 == 0 and off + n <= AB, f"arena {off}+{n}"
        v = arena[:, off:off + n].bitcast(dtype)
        if len(shape) > 1:
            lets = "abcd"[: len(shape)]
            v = v.rearrange(f"p ({' '.join(lets)}) -> p {' '.join(lets)}",
                            **{l: s for l, s in zip(lets, shape)})
        return v

    x32 = av(0, [DCH, TBA], F32)
    x_bf = av(16384, [DCH, TBA], BF16)
    XIW = 520
    x_in = av(24576, [DCH, XIW], BF16)
    silres = av(33024, [DCH, TBA], BF16)
    xc_bf = av(41216, [DCH, TBA], BF16)
    delta = av(49408, [DCH, TBA], F32)
    dA = av(65792, [2, DCH, TBA], BF16)
    y3_bf = av(65792, [DCH, TBA], BF16)
    dBu = av(82176, [DCH, TBA], BF16)
    tmpn = av(98560, [DCH, TBA], BF16)
    x1blk = av(82176, [DCH, TBA], BF16)
    ys = av(90368, [DCH, TBA], BF16)
    y_acc = av(98560, [DCH, TBA], F32)
    ln_t = av(98560, [DCH, TBA], F32)
    scr32 = av(82176, [2, TBA], F32)
    conv_acc = av(90368, [2, TBA], BF16)
    fca = av(99200, [2, TBF], BF16)
    sq_bf = av(24576, [DCH, TBA], BF16)

    X1W = 520
    x1sl = [av(0, [DCH, X1W], BF16),
            av(8320, [DCH, X1W], BF16)]
    btm_f = av(33280, [2, X1W], BF16)
    x2o = av(58240, [DCH, TBF], F32)
    Xq = av(16640, [DCH, X1W], BF16)
    HSW = 520
    hs = av(24960, [HCH, HSW], BF16)
    hsil = av(58240, [HCH, TBF], BF16)
    w1s = av(91008, [2, DCH, 128], BF16)
    w2s = av(95104, [2, DCH, 128], BF16)
    w2c = av(91008, [2, 16, 128], BF16)
    r2 = av(99200, [DCH, TBF], F32)
    r2f = av(99200, [DCH * TBF], F32)
    tqg = av(99200, [4, X1W], F32)
    btm = av(107520, [2, X1W], BF16)
    qz16 = av(107520, [4, X1W], F16)
    qi16 = av(111680, [4, X1W], mybir.dt.int16)
    qscr = av(16640, [3, 512], F32)

    q32bf = av(33280, [520], BF16)
    wqs32 = av(0, [2, 4096], F32)
    wqbf = av(32768, [2, 4096], BF16)
    wq16b = av(49152, [2, 4096], BF16)
    wqprt = av(65536, [40], F32)
    wq2 = av(65792, [40], F32)
    wq2bf = av(66048, [40], BF16)
    sfold = av(66176, [96], F32)
    lnm = av(49408, [TBA], F32)
    lnv = av(51456, [TBA], F32)
    lnr = av(53504, [TBA], F32)
    lnmr = av(55552, [TBA], F32)
    l2m = av(16640, [TBF], F32)
    l2v = av(18688, [TBF], F32)
    l2r = av(20736, [TBF], F32)
    l2mr = av(22784, [TBF], F32)

    BCB = 12544
    bca = sbt("bca", [128, BCB], U8)

    def bv(off, shape, dtype):
        n = int(np.prod(shape)) * mybir.dt.size(dtype)
        assert off + n <= BCB
        v = bca[:, off:off + n].bitcast(dtype)
        if len(shape) > 1:
            lets = "abcd"[: len(shape)]
            v = v.rearrange(f"p ({' '.join(lets)}) -> p {' '.join(lets)}",
                            **{l: s for l, s in zip(lets, shape)})
        return v

    Bp_b = bv(0, [4, TBA], BF16)
    Cp_b = bv(4160, [4, TBA], BF16)
    s1_b = bv(8320, [X1W], F32)
    e1_b = bv(10400, [X1W], F32)
    s2_b = bv(0, [TBF], F32)
    e2_b = bv(2080, [TBF], F32)
    hstash = bv(12480, [HCH], BF16)

    psb = [ctx.enter_context(nc.psum_tensor(f"psb{i}", [128, 512], F32))
           for i in range(8)]
    ps_la = psb[6]

    sems = {k: ctx.enter_context(nc.semaphore(k))
            for k in ("ld", "so", "sv", "ss", "st", "sg")}
    P = Prog(nc, sems)
    ev = {"blk_free": {"sv": 0, "so": 0}, "la_prev": 0}

    def dma(dst, src):
        return lambda e: e.dma_start(out=dst, in_=src)

    def bcast(dst, src_row, width):
        src = src_row.unsqueeze(1).broadcast_to([1, 128, width])
        return lambda e: e.dma_start(out=dst, in_=src)

    def brd(vec, n, w):
        return vec[:, 0:w].unsqueeze(1).broadcast_to([128, n, w])

    w = P.op("sync", dma(ident128[:], identh[:]), inc="ld")
    P.op("vector", lambda e: e.tensor_copy(identbf[:], ident128[:]),
         waits=[("ld", w["ld"])], inc="sv")
    for dst, src in [(p_bin, b_in), (p_cw1, cw1), (p_cb1, cb1), (p_bxp, bxp),
                     (p_bdt, bdt), (p_A, alog), (p_D, dpar), (p_bos, b_out),
                     (p_bsk, b_sk), (p_cw2, cw2), (p_cb2, cb2)]:
        P.op("sync", dma(dst[:], src[:]), inc="ld")
    ld_small = P.cur("ld")

    def w_cast(dst, src_c, nchunks, width):
        cev = {}
        for c in range(nchunks):
            stg = wqs32[:, c % 2, 0:width]
            wts = ([("sv", P.cur("sv"))] if c < 2
                   else [("sv", cev[c - 2])])
            w = P.op("sync", (lambda s_, cc: dma(s_, src_c[cc]))(stg, c),
                     waits=wts, inc="ld")
            v = P.op("vector",
                     (lambda d_, wdt: lambda e: e.tensor_copy(d_, wdt))
                     (dst[:, c, :], stg),
                     waits=[("ld", w["ld"])], inc="sv")
            cev[c] = v["sv"]

    w_cast(w_in, winT.rearrange("(c p) o -> c p o", p=128), DCH, 2 * D)
    w_cast(w_out, woutT.rearrange("(c p) o -> c p o", p=128), DCH, D)
    w_cast(w_sk, wskT.rearrange("(c p) o -> c p o", p=128), DCH, D)
    w_cast(w_xp, wxpT.rearrange("(c p) o -> c p o", p=128), DCH, 96)
    w = P.op("sync", dma(wqs32[0:64, 0, 0:D], wdtT[:]),
             waits=[("sv", P.cur("sv"))], inc="ld")
    P.op("vector", lambda e: e.tensor_copy(w_dt[:], wqs32[0:64, 0, 0:D]),
         waits=[("ld", w["ld"])], inc="sv")

    P.op("vector", lambda e: e.memset(ones_bf[:], 1.0), waits=[("ld", ld_small)])
    P.op("vector", lambda e: e.memset(ones128[:], 1.0))
    P.op("vector", lambda e: e.memset(p_eps[:], 1e-5))
    P.op("vector", lambda e: e.memset(p_half[:, 0:1], 0.5))
    P.op("vector", lambda e: e.memset(p_half[:, 1:2], -0.5))
    P.op("vector", lambda e: e.memset(sfold[:, 64:96], 0.0))
    P.op("vector", lambda e: e.memset(qscr[:, 2, 0:128], 0.0))
    P.op("vector", lambda e: e.tensor_tensor(p_bos[:], p_bos[:], p_bsk[:],
                                             OP.add))
    P.op("scalar", lambda e: e.activation(p_A[:], p_A[:], AF.Exp),
         waits=[("ld", ld_small)], inc="ss")
    P.op("vector", lambda e: e.tensor_scalar(p_A[:], p_A[:], -1.0, None,
                                             OP.mult),
         waits=[("ss", P.cur("ss"))], inc="sv")

    w1rows = w1T.rearrange("(c p) h -> c p h", p=128)
    w2rows = w2T.rearrange("(c p) dd -> c p dd", p=128)

    def qpass1(rows, nchunks, width, prt0):
        rev = {}
        for c in range(nchunks):
            stg = wqs32[:, c % 2, 0:width]
            wts = ([("sv", P.cur("sv"))] if c < 2
                   else [("sv", rev[c - 2])])
            w = P.op("sync", (lambda s_, cc: dma(s_, rows[cc]))(stg, c),
                     waits=wts, inc="ld")
            P.op("vector", (lambda s_, cc: lambda e: e.tensor_reduce(
                wqprt[:, prt0 + cc:prt0 + cc + 1], s_, AX.X,
                OP.add, apply_absolute_value=True))(stg, c),
                waits=[("ld", w["ld"])])
            v = P.op("vector",
                     lambda e: e.tensor_copy(sfold[:, 64:80],
                                             sfold[:, 80:96]), inc="sv")
            rev[c] = v["sv"]

    qpass1(w1rows, DCH, H, 0)
    qpass1(w2rows, HCH, D, 8)
    w = P.op("sync", dma(wq2[:, 0:40], wqprt[:, 0:40]),
             waits=[("sv", P.cur("sv"))], inc="ld")
    v = P.op("vector", lambda e: e.tensor_copy(wq2bf[:, 0:40], wq2[:, 0:40]),
             waits=[("ld", w["ld"])], inc="sv")
    t = P.op("tensor", lambda e: e.matmul(psb[4][0:1, 0:40], ones_bf[:],
                                          wq2bf[:, 0:40], start=True,
                                          stop=True),
             waits=[("sv", v["sv"])], inc="st")

    def sspacer(e):
        return e.tensor_copy(sfold[:, 64:80], sfold[:, 80:96])

    f = sfold[0:1, 0:64]
    P.op("vector", lambda e: e.tensor_copy(f[:, 16:56], psb[4][0:1, 0:40]),
         waits=[("st", t["st"])])
    P.op("vector", sspacer)
    P.op("vector", lambda e: e.tensor_tensor(f[:, 0:4], f[:, 16:20],
                                             f[:, 20:24], OP.add))
    P.op("vector", sspacer)
    P.op("vector", lambda e: e.tensor_tensor(f[:, 4:6], f[:, 0:2], f[:, 2:4],
                                             OP.add))
    P.op("vector", sspacer)
    P.op("vector", lambda e: e.tensor_tensor(swv[:, 0:1], f[:, 4:5],
                                             f[:, 5:6], OP.add))
    P.op("vector", sspacer)
    P.op("vector", lambda e: e.tensor_tensor(f[:, 8:24], f[:, 24:40],
                                             f[:, 40:56], OP.add))
    P.op("vector", sspacer)
    P.op("vector", lambda e: e.tensor_tensor(f[:, 24:32], f[:, 8:16],
                                             f[:, 16:24], OP.add))
    P.op("vector", sspacer)
    P.op("vector", lambda e: e.tensor_tensor(f[:, 0:4], f[:, 24:28],
                                             f[:, 28:32], OP.add))
    P.op("vector", sspacer)
    P.op("vector", lambda e: e.tensor_tensor(f[:, 4:6], f[:, 0:2], f[:, 2:4],
                                             OP.add))
    P.op("vector", sspacer)
    P.op("vector", lambda e: e.tensor_tensor(swv[:, 1:2], f[:, 4:5],
                                             f[:, 5:6], OP.add))
    P.op("vector", sspacer)
    P.op("vector", lambda e: e.tensor_scalar(swv[:, 0:2], swv[:, 0:2],
                                             1.0 / (D * H), 1e-5, OP.mult,
                                             OP.max))
    P.op("vector", sspacer)
    P.op("vector", lambda e: e.reciprocal(swv[:, 2:4], swv[:, 0:2]))
    P.op("vector", sspacer)
    sw = P.op("vector", lambda e: e.tensor_scalar(swv[:, 4:6], swv[:, 0:2],
                                                  1.0 / 254, None, OP.mult),
              inc="sv")
    bw = P.op("sync", bcast(swrep[:, 0:4], swv[0:1, 2:6], 4),
              waits=[("sv", sw["sv"])], inc="ld")
    sw_ld = bw["ld"]

    def qpass2(rows, dst_fn, nchunks, width, sw_col):
        sig_ev = {}
        st_ev = {}
        for c in range(nchunks):
            stg = wqs32[:, c % 2, 0:width]
            bfb = wqbf[:, c % 2, 0:width]
            s16 = wq16b[:, c % 2, 0:width]
            wts = [("ld", sw_ld)]
            if c < 2:
                wts += [("sv", P.cur("sv")), ("so", P.cur("so")),
                        ("ss", P.cur("ss"))]
            else:
                wts.append(("ss", sig_ev[c - 2]))
            w = P.op("sync", (lambda s_, cc: dma(s_, rows[cc]))(stg, c),
                     waits=wts, inc="ld")
            v = P.op("vector", (lambda s_: lambda e: e.tensor_scalar(
                s_, s_, swrep[:, sw_col:sw_col + 1], None, OP.mult))(stg),
                waits=[("ld", w["ld"])], inc="sv")
            swts = [("sv", v["sv"])]
            swts.append(("so", st_ev[c - 2]) if c >= 2
                        else ("so", P.cur("so")))
            s = P.op("scalar", (lambda b_, s_: lambda e: e.sign(
                b_, s_, bias=-0.5))(bfb, stg), waits=swts)
            s = P.op("scalar", (lambda b_, s_: lambda e: e.sign(
                b_, s_, bias=0.5))(s16, stg), inc="ss")
            sig_ev[c] = s["ss"]
            v = P.op("vector", (lambda b_, s_: lambda e: e.tensor_tensor(
                b_, b_, s_, OP.add))(bfb, s16),
                waits=[("ss", s["ss"])], inc="sv")
            st = P.op("gpsimd", (lambda b_, cc: dma(dst_fn(cc), b_))(bfb, c),
                      waits=[("sv", v["sv"])], inc="so")
            st_ev[c] = st["so"]

    qpass2(w1rows, lambda c: w1q_h[:, c * 128:(c + 1) * 128, :]
           .transpose([1, 0, 2]), DCH, H, 0)
    qpass2(w2rows, lambda c: w2q_h.rearrange("(c p) dd -> c p dd", p=128)[c],
           HCH, D, 1)
    ev["wq_done"] = P.cur("so")
    ev["blk_free"] = {"sv": P.cur("sv"), "so": P.cur("so")}


    def emit_quant_scales(tmax, W, blkw, swcol, s_dst, e_dst, fold_ev):
        npc = blkw // 128

        def spacer(e):
            return e.tensor_copy(qscr[:, 2, 0:64], qscr[:, 2, 64:128])

        for pc in range(npc):
            g = P.op("tensor", (lambda p_: lambda e: e.transpose(
                psb[4][:, p_ * 128:(p_ + 1) * 128],
                tmax[:, p_ * 128:(p_ + 1) * 128], ident128[:]))(pc),
                waits=[("sv", fold_ev)] if pc == 0 else (),
                inc="st" if (pc == npc - 1 and W == blkw) else None)
        if W > blkw:
            g = P.op("tensor", lambda e: e.transpose(
                psb[5][0:1, 0:128], tmax[:, blkw:blkw + 1], ident128[:]),
                inc="st")
        b0 = qscr[:, 0, 0:512].rearrange("p (a b) -> p a b", a=npc)
        P.op("vector", lambda e: e.tensor_copy(
            b0, psb[4][:, 0:npc * 128].rearrange("p (a b) -> p a b", a=npc)),
            waits=[("st", g["st"])])
        P.op("vector", spacer)
        tA = qscr[:, 1, 0:npc * 64].rearrange("p (a b) -> p a b", a=npc)
        tB = qscr[:, 1, 256:256 + npc * 32].rearrange("p (a b) -> p a b", a=npc)
        tC = qscr[:, 1, 384:384 + npc * 16].rearrange("p (a b) -> p a b", a=npc)
        t4 = qscr[:, 1, 448:448 + npc * 8].rearrange("p (a b) -> p a b", a=npc)
        t5 = qscr[:, 1, 480:480 + npc * 4].rearrange("p (a b) -> p a b", a=npc)
        t6 = qscr[:, 1, 496:496 + npc * 2].rearrange("p (a b) -> p a b", a=npc)
        steps = [(tA, b0, 64), (tB, tA, 32), (tC, tB, 16), (t4, tC, 8),
                 (t5, t4, 4), (t6, t5, 2)]
        for out_v, in_v, w_ in steps:
            P.op("vector", (lambda o_, i_, ww: lambda e: e.tensor_tensor(
                o_, i_[:, :, 0:ww], i_[:, :, ww:2 * ww], OP.max))
                (out_v, in_v, w_))
            P.op("vector", spacer)
        mxq = qscr[:, 1, 504:504 + npc]
        P.op("vector", lambda e: e.tensor_tensor(
            mxq.rearrange("p (a b) -> p a b", a=npc), t6[:, :, 0:1],
            t6[:, :, 1:2], OP.max))
        P.op("vector", spacer)
        mxc = qscr[:, 1, 508:508 + npc]
        sev_v = qscr[:, 0, 0:2 * npc]
        P.op("vector", lambda e: e.tensor_scalar(mxc, mxq, 1e-5, None, OP.max))
        P.op("vector", spacer)
        P.op("vector", lambda e: e.reciprocal(sev_v[:, 0:npc], mxc))
        P.op("vector", spacer)
        P.op("vector", lambda e: e.tensor_scalar(sev_v[:, 0:npc],
                                                 sev_v[:, 0:npc], 127.0, None,
                                                 OP.mult))
        P.op("vector", spacer)
        v = P.op("vector", lambda e: e.tensor_scalar(
            sev_v[:, npc:2 * npc], mxc, swrep[:, 2 + swcol:3 + swcol], None,
            OP.mult), inc="sv")
        g = P.op("tensor", lambda e: e.transpose(
            psb[5][0:2 * npc, 128:256], sev_v, ident128[:]),
            waits=[("sv", v["sv"])], inc="st")
        v = P.op("vector", lambda e: e.tensor_copy(
            qscr[0:2 * npc, 0, 16:144], psb[5][0:2 * npc, 128:256]),
            waits=[("st", g["st"])], inc="sv")
        sev = v["sv"]
        if W > blkw:
            c = qscr[0:1, 0, 160:352]
            P.op("vector", lambda e: e.tensor_copy(c[:, 0:128],
                                                   psb[5][0:1, 0:128]))
            P.op("vector", spacer)
            for (o0, o1, i0, ww) in [(128, 192, 0, 64), (0, 32, 128, 32),
                                     (32, 48, 0, 16), (48, 56, 32, 8),
                                     (56, 60, 48, 4), (60, 62, 56, 2),
                                     (62, 63, 60, 1)]:
                P.op("vector", (lambda a_, b_, c_, ww_: lambda e:
                     e.tensor_tensor(c[:, a_:b_], c[:, c_:c_ + ww_],
                                     c[:, c_ + ww_:c_ + 2 * ww_], OP.max))
                     (o0, o1, i0, ww))
                P.op("vector", spacer)
            P.op("vector", lambda e: e.tensor_scalar(c[:, 64:65], c[:, 62:63],
                                                     1e-5, None, OP.max))
            P.op("vector", spacer)
            P.op("vector", lambda e: e.reciprocal(c[:, 65:66], c[:, 64:65]))
            P.op("vector", spacer)
            P.op("vector", lambda e: e.tensor_scalar(c[:, 65:66], c[:, 65:66],
                                                     127.0, None, OP.mult))
            P.op("vector", spacer)
            v = P.op("vector", (lambda sc: lambda e: e.tensor_scalar(
                c[:, 66:67], c[:, 64:65], swv[0:1, 4 + sc:5 + sc], None,
                OP.mult))(swcol), inc="sv")
            sev = v["sv"]
        for pc in range(npc):
            P.op("sync", (lambda p_: bcast(s_dst[:, p_ * 128:(p_ + 1) * 128],
                                           qscr[p_:p_ + 1, 0, 16:144], 128))
                 (pc), waits=[("sv", sev)] if pc == 0 else (), inc="ld")
            w = P.op("sync", (lambda p_: bcast(
                e_dst[:, p_ * 128:(p_ + 1) * 128],
                qscr[npc + p_:npc + p_ + 1, 0, 16:144], 128))(pc), inc="ld")
        if W > blkw:
            P.op("sync", bcast(s_dst[:, blkw:blkw + 1],
                               qscr[0:1, 0, 225:226], 1), inc="ld")
            w = P.op("sync", bcast(e_dst[:, blkw:blkw + 1],
                                   qscr[0:1, 0, 226:227], 1), inc="ld")
        return w["ld"]

    def emit_attn_block(b, j):
        t0 = j * TBA
        f = ev["blk_free"]
        w = P.op("sync", dma(x32[:], xT[b].rearrange("(c p) t -> p c t", p=128)
                             [:, :, t0:t0 + TBA]),
                 waits=[("sv", f["sv"]), ("so", f["so"])], inc="ld")
        x_ld = w["ld"]
        if j == 0:
            h0 = P.op("vector", lambda e: e.memset(x_in[:, :, 1:4], 0.0),
                      inc="sv")
        else:
            h0 = P.op("vector", lambda e: e.tensor_copy(
                x_in[:, :, 1:4],
                xtail[:].rearrange("p (a b) -> p a b", a=DCH)), inc="sv")
        halo_ev = h0["sv"]
        xbf_ev = {}
        for c in range(DCH):
            v = P.op("vector", (lambda cc: lambda e: e.tensor_copy(
                x_bf[:, cc, :], x32[:, cc, :]))(c),
                waits=[("ld", x_ld)] if c == 0 else (), inc="sv")
            xbf_ev[c] = v["sv"]

        evac_ev = {}
        for o in range(16):
            bank = psb[o % 3]
            for k in range(DCH):
                wts = []
                if o == 0:
                    wts.append(("sv", xbf_ev[k]))
                elif k == 0 and o >= 3:
                    wts.append(("ss", evac_ev[o - 3]))
                g = P.op("tensor", (lambda bk, oo, kk: lambda e: e.matmul(
                    bk[:], w_in[:, kk, oo * 128:(oo + 1) * 128], x_bf[:, kk, :],
                    start=(kk == 0), stop=(kk == DCH - 1)))(bank, o, k),
                    waits=wts,
                    inc="st" if k == DCH - 1 else None)
            if o < 8:
                s = P.op("scalar", (lambda bk, oo: lambda e: e.activation(
                    x_in[:, oo, 4:516], bk[:], AF.Identity,
                    bias=p_bin[:, oo:oo + 1]))(bank, o),
                    waits=[("st", g["st"]), ("sv", halo_ev)], inc="ss")
            else:
                s = P.op("scalar", (lambda bk, oo: lambda e: e.activation(
                    silres[:, oo - 8, :], bk[:], AF.Silu,
                    bias=p_bin[:, oo:oo + 1]))(bank, o),
                    waits=[("st", g["st"])], inc="ss")
            evac_ev[o] = s["ss"]

        xc_ev = {}
        v3_ev = {}
        for c in range(DCH):
            ca = conv_acc[:, c % 2, :]
            wts = [("ss", evac_ev[c])]
            if c >= 2:
                wts.append(("ss", xc_ev[c - 2]))
            P.op("vector", (lambda cc, ca_: lambda e: e.tensor_scalar(
                ca_, x_in[:, cc, 1:513],
                p_cw1[:, cc * 4:cc * 4 + 1], None, OP.mult))(c, ca),
                waits=wts)
            P.op("vector", (lambda cc, ca_: lambda e: e.scalar_tensor_tensor(
                ca_, x_in[:, cc, 2:514],
                p_cw1[:, cc * 4 + 1:cc * 4 + 2], ca_,
                OP.mult, OP.add))(c, ca))
            P.op("vector", (lambda cc, ca_: lambda e: e.scalar_tensor_tensor(
                ca_, x_in[:, cc, 3:515],
                p_cw1[:, cc * 4 + 2:cc * 4 + 3], ca_,
                OP.mult, OP.add))(c, ca))
            v = P.op("vector", (lambda cc, ca_: lambda e:
                     e.scalar_tensor_tensor(
                         ca_, x_in[:, cc, 4:516],
                         p_cw1[:, cc * 4 + 3:cc * 4 + 4], ca_,
                         OP.mult, OP.add))(c, ca), inc="sv")
            v3_ev[c] = v["sv"]
            s = P.op("scalar", (lambda cc, ca_: lambda e: e.activation(
                xc_bf[:, cc, :], ca_, AF.Silu,
                bias=p_cb1[:, cc:cc + 1]))(c, ca),
                waits=[("sv", v["sv"])], inc="ss")
            xc_ev[c] = s["ss"]

        for k in range(DCH):
            g = P.op("tensor", (lambda kk: lambda e: e.matmul(
                psb[3][0:96, :], w_xp[:, kk, :], xc_bf[:, kk, :],
                start=(kk == 0), stop=(kk == DCH - 1)))(k),
                waits=[("ss", xc_ev[k])],
                inc="st" if k == DCH - 1 else None)
        s = P.op("scalar", lambda e: e.activation(ssm32[:], psb[3][0:96, :],
                                                  AF.Identity, bias=p_bxp[:, 0:1]),
                 waits=[("st", g["st"])], inc="ss")
        P.op("vector", lambda e: e.tensor_copy(bpcp[:], ssm32[64:96, :]),
             waits=[("ss", s["ss"]), ("ld", ev.get("bc_last", 0))])
        v = P.op("vector", lambda e: e.tensor_copy(dt_bf[:], ssm32[0:64, :]),
                 inc="sv")
        ssmc_ev = v["sv"]

        dsp_ev = {}
        for o in range(DCH):
            bank = psb[o % 3]
            wts = [("sv", ssmc_ev)]
            if o >= 3:
                wts.append(("ss", dsp_ev[o - 3]))
            g = P.op("tensor", (lambda bk, oo: lambda e: e.matmul(
                bk[:], w_dt[:, oo * 128:(oo + 1) * 128], dt_bf[:],
                start=True, stop=True))(bank, o), waits=wts, inc="st")
            P.op("scalar", (lambda bk, oo: lambda e: e.activation(
                scr32[:, 0, :], bk[:], AF.Exp, bias=p_bdt[:, oo:oo + 1]))
                (bank, o), waits=[("st", g["st"])])
            s = P.op("scalar", (lambda oo: lambda e: e.activation(
                delta[:, oo, :], scr32[:, 0, :], AF.Ln, bias=1.0))(o),
                inc="ss")
            dsp_ev[o] = s["ss"]

        P.op("vector", lambda e: e.tensor_copy(
            xtail[:].rearrange("p (a b) -> p a b", a=DCH),
            x_in[:, :, 513:516]), waits=[("ss", xc_ev[DCH - 1])])
        u_view = x_in[:, :, 4:516]
        v = P.op("vector", lambda e: e.tensor_tensor(u_view, delta[:], xc_bf[:],
                                                     OP.mult),
                 waits=[("ss", dsp_ev[DCH - 1])], inc="sv")
        u_ev = v["sv"]

        if j == 0:
            P.op("vector", lambda e: e.memset(carry[:], 0.0))
        scan_ev = {}
        gadd_ev = {}
        for n in range(N16):
            h = n % 2
            h4 = n % 4
            wts = [("sv", ssmc_ev)]
            if n >= 4:
                wts.append(("sv", scan_ev[n - 4]))
            P.op("sync", bcast(Bp_b[:, h4, :], bpcp[n:n + 1, :], TBA),
                 waits=wts, inc="ld")
            w = P.op("sync", bcast(Cp_b[:, h4, :], bpcp[16 + n:17 + n, :], TBA),
                     inc="ld")
            bc_ld = w["ld"]
            ev["bc_last"] = bc_ld
            wts = []
            if n >= 2:
                wts.append(("sv", scan_ev[n - 2]))
            s = P.op("scalar", (lambda nn, hh: lambda e: e.activation(
                dA[:, hh], delta[:], AF.Exp, scale=p_A[:, nn:nn + 1]))(n, h),
                waits=wts, inc="ss")
            dA_ev = s["ss"]
            wts = [("ld", bc_ld), ("sv", u_ev)]
            P.op("vector", (lambda hh: lambda e: e.tensor_tensor(
                dBu[:], u_view,
                Bp_b[:, hh, :].unsqueeze(1).broadcast_to([128, DCH, TBA]),
                OP.mult))(h4), waits=wts)
            for c in range(DCH):
                P.op("vector", (lambda nn, hh, cc: lambda e:
                     e.tensor_tensor_scan(
                         ys[:, cc, :], dA[:, hh, cc, :], dBu[:, cc, :],
                         carry[:, nn * DCH + cc:nn * DCH + cc + 1],
                         OP.mult, OP.add))(n, h, c),
                    waits=[("ss", dA_ev)] if c == 0 else ())
            v = P.op("vector", (lambda hh: lambda e: e.tensor_tensor(
                tmpn[:], ys[:],
                Cp_b[:, hh, :].unsqueeze(1).broadcast_to([128, DCH, TBA]),
                OP.mult))(h4),
                waits=[("st", gadd_ev[n - 1])] if n >= 1 else (), inc="sv")
            tmp_ev = v["sv"]
            for c in range(DCH):
                g = P.op("tensor", (lambda nn, cc: lambda e: e.matmul(
                    psb[cc][:], identbf[:], tmpn[:, cc, :],
                    start=(nn == 0), stop=(nn == N16 - 1),
                    skip_group_check=True))(n, c),
                    waits=[("sv", tmp_ev)] if c == 0 else (),
                    inc="st" if c == DCH - 1 else None)
            gadd_ev[n] = g["st"]
            v = P.op("vector", (lambda nn: lambda e: e.tensor_copy(
                carry[:, nn * DCH:(nn + 1) * DCH], ys[:, :, TBA - 1]))(n),
                inc="sv")
            scan_ev[n] = v["sv"]

        for c in range(DCH):
            P.op("vector", (lambda cc: lambda e: e.scalar_tensor_tensor(
                y_acc[:, cc, :], xc_bf[:, cc, :], p_D[:, cc:cc + 1],
                psb[cc][:], OP.mult, OP.add))(c),
                waits=[("st", gadd_ev[N16 - 1])] if c == 0 else ())
        if debug:
            vd = P.op("vector", lambda e: e.tensor_copy(y_acc[:, 0, 0:1],
                                                        y_acc[:, 0, 0:1]),
                      inc="sv")
            P.op("sync", dma(dbg_h["d_y"].rearrange("(c p) t -> p c t", p=128)
                             [:, :, t0:t0 + TBA], y_acc[:]),
                 waits=[("sv", vd["sv"])], inc="so")
        v = P.op("vector", lambda e: e.tensor_tensor(y3_bf[:], y_acc[:],
                                                     silres[:], OP.mult),
                 inc="sv")
        y3_ev = v["sv"]

        rres_ev = {}
        for o in range(DCH):
            bank = psb[o % 3]
            wts = [("sv", y3_ev)]
            if o >= 3:
                wts.append(("sv", rres_ev[o - 3]))
            for k in range(DCH):
                P.op("tensor", (lambda bk, oo, kk: lambda e: e.matmul(
                    bk[:], w_out[:, kk, oo * 128:(oo + 1) * 128],
                    y3_bf[:, kk, :], start=(kk == 0), stop=False))(bank, o, k),
                    waits=wts if k == 0 else ())
            for k in range(DCH):
                g = P.op("tensor", (lambda bk, oo, kk: lambda e: e.matmul(
                    bk[:], w_sk[:, kk, oo * 128:(oo + 1) * 128], x_bf[:, kk, :],
                    start=False, stop=(kk == DCH - 1)))(bank, o, k),
                    inc="st" if k == DCH - 1 else None)
            v = P.op("vector", (lambda bk, oo: lambda e:
                     e.scalar_tensor_tensor(
                         x32[:, oo, :], bk[:], p_bos[:, oo:oo + 1],
                         x32[:, oo, :], OP.add, OP.add))(bank, o),
                 waits=[("st", g["st"])], inc="sv")
            rres_ev[o] = v["sv"]
        r_ev = rres_ev[DCH - 1]
        if debug:
            P.op("sync", dma(dbg_h["d_res"].rearrange("(c p) t -> p c t", p=128)
                             [:, :, t0:t0 + TBA], x32[:]),
                 waits=[("sv", r_ev)], inc="so")
            P.op("sync", dma(dbg_h["d_xc"].rearrange("(c p) t -> p c t", p=128)
                             [:, :, t0:t0 + TBA], xc_bf[:]),
                 waits=[("sv", r_ev)], inc="so")
            P.op("sync", dma(dbg_h["d_delta"]
                             .rearrange("(c p) t -> p c t", p=128)
                             [:, :, t0:t0 + TBA], delta[:]),
                 waits=[("sv", r_ev)], inc="so")
            P.op("sync", dma(dbg_h["d_ssm"][:, t0:t0 + TBA], ssm32[:]),
                 waits=[("sv", r_ev)], inc="so")

        s = P.op("scalar", lambda e: e.activation(sq_bf[:, :, 0:TBA], x32[:],
                                                  AF.Square),
                 waits=[("sv", r_ev)], inc="ss")
        sq_ev = s["ss"]
        v = P.op("vector", lambda e: e.tensor_copy(x_bf[:], x32[:]),
                 waits=[("sv", r_ev)], inc="sv")
        rbf_ev = v["sv"]
        for k in range(DCH):
            P.op("tensor", (lambda kk: lambda e: e.matmul(
                psb[4][:], ones128[:], x_bf[:, kk, :],
                start=(kk == 0), stop=(kk == DCH - 1)))(k),
                waits=[("sv", rbf_ev)] if k == 0 else ())
        for k in range(DCH):
            g = P.op("tensor", (lambda kk: lambda e: e.matmul(
                psb[5][:], ones128[:], sq_bf[:, kk, 0:TBA],
                start=(kk == 0), stop=(kk == DCH - 1)))(k),
                waits=[("ss", sq_ev)] if k == 0 else (),
                inc="st" if k == DCH - 1 else None)
        P.op("vector", lambda e: e.tensor_scalar(lnm[:], psb[4][:],
                                                 1.0 / D, None, OP.mult),
             waits=[("st", g["st"])])
        P.op("vector", lambda e: e.tensor_scalar(lnv[:], psb[5][:],
                                                 1.0 / D, None, OP.mult))
        P.op("vector", lambda e: e.tensor_tensor(lnr[:], lnm[:], lnm[:],
                                                 OP.mult))
        v = P.op("vector", lambda e: e.tensor_tensor(
            lnv[:], lnv[:], lnr[:], OP.subtract), inc="sv")
        P.op("scalar", lambda e: e.activation(lnr[:], lnv[:], AF.Ln,
                                              bias=1e-5),
             waits=[("sv", v["sv"])])
        s = P.op("scalar", lambda e: e.activation(lnr[:], lnr[:],
                                                  AF.Exp, scale=-0.5),
                 inc="ss")
        P.op("vector", lambda e: e.tensor_tensor(
            lnmr[:], lnm[:], lnr[:], OP.mult), waits=[("ss", s["ss"])])
        P.op("vector", lambda e: e.tensor_tensor(
            ln_t[:], x32[:], brd(lnr, DCH, TBA), OP.mult))
        v = P.op("vector", lambda e: e.tensor_tensor(
            x1blk[:], ln_t[:], brd(lnmr, DCH, TBA), OP.subtract), inc="sv")
        x1_ev = v["sv"]
        st = P.op("gpsimd", dma(x1_h[b].rearrange("(c p) t -> p c t", p=128)
                                [:, :, t0:t0 + TBA], x1blk[:]),
                  waits=[("sv", x1_ev)], inc="so")
        if debug:
            P.op("sync", dma(dbg_h["d_x1"].rearrange("(c p) t -> p c t", p=128)
                             [:, :, t0:t0 + TBA], x1blk[:]),
                 waits=[("sv", x1_ev)], inc="so")
        ev["blk_free"] = {"sv": x1_ev, "so": st["so"]}
        ev[f"x1_{b}"] = st["so"]

    def emit_ffn_load(b, j):
        t0 = j * TBF
        lastb = (j == NBF - 1)
        ncols = 512 if lastb else 513
        x1s = x1sl[j % 2]
        w = P.op("sync", (lambda x_, tt, nn: dma(
            x_[:, :, 0:nn], x1_h[b].rearrange("(c p) t -> p c t", p=128)
            [:, :, tt:tt + nn]))(x1s, t0, ncols),
            waits=[("so", ev[f"x1_{b}"]),
                   ("sv", ev.get(f"x1f_free{j % 2}", 0))], inc="ld")
        ev["x1f_ld"] = w["ld"]

    def emit_ffn_trees(b, j):
        lastb = (j == NBF - 1)
        W = 513
        x1s = x1sl[j % 2]
        x1_ld = ev["x1f_ld"]
        if j > 0:
            sv0 = P.op("vector", lambda e: e.tensor_copy(
                hstash[:], hs[:, :, 512:513].rearrange("p a b -> p (a b)")),
                inc="sv")
            if debug and b == 0 and j == 1:
                P.op("sync", dma(dbg_h["d_hst"][:, 0:HCH], hstash[:]),
                     waits=[("sv", sv0["sv"])], inc="so")
        if lastb:
            P.op("vector", (lambda x_: lambda e: e.memset(
                x_[:, :, 512:513], 0.0))(x1s), waits=[("ld", x1_ld)])
        tmaxb = btm_f[:, 0, 0:W]
        tminb = btm_f[:, 1, 0:W]
        P.op("vector", (lambda x_: lambda e: e.tensor_tensor(
            tmaxb, x_[:, 0, 0:W], x_[:, 1, 0:W], OP.max))(x1s),
            waits=[("ld", x1_ld)])
        P.op("vector", (lambda x_: lambda e: e.tensor_tensor(
            tminb, x_[:, 0, 0:W], x_[:, 1, 0:W], OP.min))(x1s))
        for c in range(2, DCH):
            P.op("vector", (lambda x_, cc: lambda e: e.tensor_tensor(
                tmaxb, tmaxb, x_[:, cc, 0:W], OP.max))(x1s, c))
            P.op("vector", (lambda x_, cc: lambda e: e.tensor_tensor(
                tminb, tminb, x_[:, cc, 0:W], OP.min))(x1s, c))
        P.op("vector", lambda e: e.tensor_scalar(tminb, tminb, -1.0, None,
                                                 OP.mult))
        v = P.op("vector", lambda e: e.tensor_tensor(tmaxb, tmaxb, tminb,
                                                     OP.max), inc="sv")
        ev["x1f_tr"] = v["sv"]
        if debug and b == 0 and j == 1:
            P.op("sync", dma(dbg_h["d_btmf"][:], btm_f[:]),
                 waits=[("sv", v["sv"])], inc="so")
            P.op("sync", (lambda x_: dma(dbg_h["d_x1s1"][:], x_[:]))(x1s),
                 waits=[("sv", v["sv"])], inc="so")

    def emit_ffn_block(b, j):
        t0 = j * TBF
        lastb = (j == NBF - 1)
        W = 513
        f = ev["blk_free"]
        x1s = x1sl[j % 2]

        tmax = r2f[:, 0:W]
        v = P.op("vector", lambda e: e.tensor_copy(tmax, btm_f[:, 0, 0:W]),
                 waits=[("sv", ev["x1f_tr"]), ("ld", ev.get("stats_ld", 0)),
                        ("so", f["so"])], inc="sv")
        if debug and b == 0 and j == 1:
            P.op("sync", dma(dbg_h["d_tmax1"][:, 0:W], tmax),
                 waits=[("sv", v["sv"])], inc="so")
        sb_ld = emit_quant_scales(tmax, W, 512, 0, s1_b, e1_b, v["sv"])
        if debug and b == 0 and j == 1:
            P.op("sync", dma(dbg_h["d_s1b1"][:], s1_b[:]),
                 waits=[("ld", sb_ld)], inc="so")
        ev["stats_ld"] = sb_ld
        for g4 in range(2):
            sl = slice(g4 * 4, g4 * 4 + 4)
            P.op("vector", (lambda sl_: lambda e: e.tensor_tensor(
                tqg[:, :, 0:W], x1s[:, sl_, 0:W], brd(s1_b, 4, W), OP.mult))
                (sl), waits=[("ld", sb_ld)])
            P.op("vector", lambda e: e.tensor_scalar(
                qz16[:, :, 0:W], tqg[:, :, 0:W], MAGIC, None, OP.add))
            P.op("vector", (lambda sl_: lambda e: e.tensor_scalar(
                Xq[:, sl_, 0:W], qz16[:, :, 0:W], -MAGIC, None, OP.add))(sl))
        v = P.op("vector", lambda e: e.tensor_tensor(
            Xq[:, :, 0:W], Xq[:, :, 0:W], brd(e1_b, DCH, W), OP.mult),
            inc="sv")
        xq_ev = v["sv"]
        if debug and b == 0 and j == 1:
            P.op("sync", dma(dbg_h["d_xq1"].rearrange("(c p) t -> p c t",
                                                      p=128), Xq[:]),
                 waits=[("sv", xq_ev)], inc="so")
        if debug and b == 0 and j == 0:
            P.op("sync", dma(dbg_h["d_xq"].rearrange("(c p) t -> p c t", p=128),
                             Xq[:]), waits=[("sv", xq_ev)], inc="so")
            P.op("sync", dma(dbg_h["d_s1"][:], s1_b[:]),
                 waits=[("sv", xq_ev)], inc="so")
            P.op("sync", dma(dbg_h["d_e1"][:], e1_b[:]),
                 waits=[("sv", xq_ev)], inc="so")

        if j == 0:
            hh0 = P.op("vector", lambda e: e.memset(hs[:, :, 0:1], 0.0),
                       inc="sv")
        else:
            hh0 = P.op("vector", lambda e: e.tensor_copy(
                hs[:, :, 0:1],
                hstash[:].rearrange("p (a b) -> p a b", b=1)), inc="sv")
            if debug and b == 0 and j == 1:
                hcp = P.op("vector", lambda e: e.tensor_copy(
                    hstash[:], hs[:, :, 0:1].rearrange("p a b -> p (a b)")),
                    waits=[("sv", hh0["sv"])], inc="sv")
                P.op("sync", dma(dbg_h["d_hst"][:, 2 * HCH:3 * HCH],
                                 hstash[:]),
                     waits=[("sv", hcp["sv"])], inc="so")
        hhalo_ev = hh0["sv"]
        w1mm_ev = {}
        hs_ev = {}
        v3f_ev = {}
        hsil_ev = {}

        def emit_fsilu(hh):
            s = P.op("scalar", (lambda h_: lambda e: e.activation(
                hsil[:, h_, :], fca[:, h_ % 2, :], AF.Silu,
                bias=p_cb2[:, h_:h_ + 1]))(hh),
                waits=[("sv", v3f_ev[hh])], inc="ss")
            hsil_ev[hh] = s["ss"]

        for h in range(HCH):
            wts = [("so", ev["wq_done"])]
            if h >= 2:
                wts.append(("st", w1mm_ev[h - 2]))
            else:
                wts.append(("st", ev.get("mm2_prev", 0)))
            w = P.op("sync", (lambda hh: dma(
                w1s[:, hh % 2], w1q_h[hh].rearrange("(c p) o -> p c o",
                                                    p=128)))(h),
                waits=wts, inc="ld")
            w1_ld = w["ld"]
            bank = psb[h % 3]
            wts = [("sv", xq_ev), ("ld", w1_ld)]
            if h >= 3:
                wts.append(("sv", hs_ev[h - 3]))
            for k in range(DCH):
                P.op("tensor", (lambda bk, hh, kk: lambda e: e.matmul(
                    bk[:], w1s[:, hh % 2, kk, :], Xq[:, kk, 0:512],
                    start=(kk == 0), stop=(kk == DCH - 1)))(bank, h, k),
                    waits=wts if k == 0 else ())
            for k in range(DCH):
                lwts = []
                if h == 0 and k == 0:
                    lwts = [("sv", ev["la_prev"])]
                g = P.op("tensor", (lambda hh, kk: lambda e: e.matmul(
                    ps_la[:, hh:hh + 1], w1s[:, hh % 2, kk, :],
                    Xq[:, kk, 512:513], start=(kk == 0),
                    stop=(kk == DCH - 1)))(h, k),
                    waits=lwts, inc="st" if k == DCH - 1 else None)
            w1mm_ev[h] = g["st"]
            P.op("vector", (lambda bk, hh: lambda e: e.tensor_copy(
                hs[:, hh, 1:513], bk[:]))(bank, h),
                waits=[("st", w1mm_ev[h])])
            v = P.op("vector", (lambda hh: lambda e: e.tensor_copy(
                hs[:, hh, 513:514], ps_la[:, hh:hh + 1]))(h), inc="sv")
            hs_ev[h] = v["sv"]
            if h >= 2:
                emit_fsilu(h - 2)
            ca = fca[:, h % 2, :]
            cwts = []
            if h >= 2:
                cwts.append(("ss", hsil_ev[h - 2]))
            P.op("vector", (lambda hh, ca_: lambda e: e.tensor_scalar(
                ca_, hs[:, hh, 0:512],
                p_cw2[:, hh * 3:hh * 3 + 1], None, OP.mult))(h, ca),
                waits=cwts)
            P.op("vector", (lambda hh, ca_: lambda e:
                 e.scalar_tensor_tensor(
                     ca_, hs[:, hh, 1:513],
                     p_cw2[:, hh * 3 + 1:hh * 3 + 2], ca_,
                     OP.mult, OP.add))(h, ca))
            v = P.op("vector", (lambda hh, ca_: lambda e:
                 e.scalar_tensor_tensor(
                     ca_, hs[:, hh, 2:514],
                     p_cw2[:, hh * 3 + 2:hh * 3 + 3], ca_,
                     OP.mult, OP.add))(h, ca), inc="sv")
            v3f_ev[h] = v["sv"]
        emit_fsilu(HCH - 2)
        emit_fsilu(HCH - 1)
        ev["la_prev"] = hs_ev[HCH - 1]
        if j + 1 < NBF:
            emit_ffn_load(b, j + 1)
        if debug and b == 0 and j == 0:
            P.op("sync", dma(dbg_h["d_hs"].rearrange("(c p) t -> p c t", p=128),
                             hs[:]), waits=[("ss", hsil_ev[HCH - 1])],
                 inc="so")

        if debug and b == 0 and j == 0:
            P.op("sync", dma(dbg_h["d_hsil"].rearrange("(c p) t -> p c t",
                                                       p=128), hsil[:]),
                 waits=[("ss", hsil_ev[HCH - 1])], inc="so")
        W2 = TBF
        t2maxb = btm[:, 0, 0:W2]
        t2minb = btm[:, 1, 0:W2]
        t2max = r2f[:, 0:W2]
        P.op("vector", lambda e: e.tensor_tensor(t2maxb, hsil[:, 0, :],
                                                 hsil[:, 1, :], OP.max),
             waits=[("ss", hsil_ev[1])])
        P.op("vector", lambda e: e.tensor_tensor(t2minb, hsil[:, 0, :],
                                                 hsil[:, 1, :], OP.min))
        for c in range(2, HCH):
            P.op("vector", (lambda cc: lambda e: e.tensor_tensor(
                t2maxb, t2maxb, hsil[:, cc, :], OP.max))(c),
                waits=[("ss", hsil_ev[c])])
            P.op("vector", (lambda cc: lambda e: e.tensor_tensor(
                t2minb, t2minb, hsil[:, cc, :], OP.min))(c))
        P.op("vector", lambda e: e.tensor_scalar(t2minb, t2minb, -1.0, None,
                                                 OP.mult))
        P.op("vector", lambda e: e.tensor_tensor(t2maxb, t2maxb, t2minb,
                                                 OP.max),
             waits=[("ld", ev.get("stats_ld", 0))])
        v = P.op("vector", lambda e: e.tensor_copy(t2max, t2maxb), inc="sv")
        s2_ld = emit_quant_scales(t2max, W2, 512, 1, s2_b, e2_b, v["sv"])
        ev["stats_ld"] = s2_ld
        vlast = None
        for g4 in range(8):
            sl = slice(g4 * 4, g4 * 4 + 4)
            P.op("vector", (lambda sl_: lambda e: e.tensor_tensor(
                tqg[:, :, 0:W2], hsil[:, sl_, :], brd(s2_b, 4, W2), OP.mult))
                (sl), waits=[("ld", s2_ld)])
            P.op("vector", lambda e: e.tensor_scalar(
                qz16[:, :, 0:W2], tqg[:, :, 0:W2], MAGIC, None, OP.add))
            vlast = P.op("vector", (lambda sl_: lambda e: e.tensor_scalar(
                hsil[:, sl_, :], qz16[:, :, 0:W2], -MAGIC, None, OP.add))(sl),
                inc="sv" if g4 == 7 else None)
        hq_ev = vlast["sv"]

        w2v = w2q_h.rearrange("(c p) dd -> p c dd", p=128)
        mm_ev = {}
        for dch in range(DCH):
            for half in range(2):
                idx = dch * 2 + half
                wts = [("so", ev["wq_done"])]
                if idx >= 2:
                    wts.append(("st", mm_ev[idx - 2]))
                else:
                    wts.append(("st", w1mm_ev[HCH - 1]))
                w = P.op("sync", (lambda dd, hh: dma(
                    w2c[:, hh],
                    w2v[:, hh * 16:(hh + 1) * 16,
                        dd * 128:(dd + 1) * 128]))(dch, half),
                    waits=wts, inc="ld")
                mwts = [("ld", w["ld"])]
                if idx == 0:
                    mwts.append(("sv", hq_ev))
                for k in range(16):
                    hk = half * 16 + k
                    g = P.op("tensor",
                             (lambda dd, hh, kk, hkk: lambda e: e.matmul(
                                 psb[dd][:], w2c[:, hh, kk, :],
                                 hsil[:, hkk, :], start=(hkk == 0),
                                 stop=(hkk == HCH - 1)))(dch, half, k, hk),
                             waits=mwts if k == 0 else (),
                             inc="st" if k == 15 else None)
                mm_ev[idx] = g["st"]
        mm2_ev = mm_ev[2 * DCH - 1]
        ev["mm2_prev"] = mm2_ev

        if j + 1 < NBF:
            emit_ffn_trees(b, j + 1)

        vlast = None
        for dch in range(DCH):
            P.op("vector", (lambda dd: lambda e: e.tensor_tensor(
                r2[:, dd, :], psb[dd][:], e2_b[:, 0:W2],
                OP.mult))(dch),
                waits=[("st", mm_ev[dch * 2 + 1])])
            vlast = P.op("vector", (lambda dd: lambda e: e.tensor_tensor(
                r2[:, dd, :], r2[:, dd, :], x1s[:, dd, 0:512], OP.add))(dch),
                inc="sv" if dch == DCH - 1 else None)
        r2_ev = vlast["sv"]
        ev[f"x1f_free{j % 2}"] = r2_ev

        if debug and b == 0 and j == 0:
            P.op("sync", dma(dbg_h["d_r2"].rearrange("(c p) t -> p c t", p=128),
                             r2[:]), waits=[("sv", r2_ev)], inc="so")
        s = P.op("scalar", lambda e: e.activation(hsil[:, 0:8, :], r2[:],
                                                  AF.Square),
                 waits=[("sv", r2_ev)], inc="ss")
        sq2_ev = s["ss"]
        v = P.op("vector", lambda e: e.tensor_copy(hs[:, 0:8, 0:512], r2[:]),
                 waits=[("sv", r2_ev)], inc="sv")
        r2bf_ev = v["sv"]
        for k in range(DCH):
            P.op("tensor", (lambda kk: lambda e: e.matmul(
                psb[4][:], ones128[:], hs[:, kk, 0:512],
                start=(kk == 0), stop=(kk == DCH - 1)))(k),
                waits=[("sv", r2bf_ev)] if k == 0 else ())
        for k in range(DCH):
            g = P.op("tensor", (lambda kk: lambda e: e.matmul(
                psb[5][:], ones128[:], hsil[:, kk, :],
                start=(kk == 0), stop=(kk == DCH - 1)))(k),
                waits=[("ss", sq2_ev)] if k == 0 else (),
                inc="st" if k == DCH - 1 else None)
        P.op("vector", lambda e: e.tensor_scalar(l2m[:], psb[4][:],
                                                 1.0 / D, None, OP.mult),
             waits=[("st", g["st"])])
        P.op("vector", lambda e: e.tensor_scalar(l2v[:], psb[5][:],
                                                 1.0 / D, None, OP.mult))
        P.op("vector", lambda e: e.tensor_tensor(l2r[:], l2m[:], l2m[:],
                                                 OP.mult))
        v = P.op("vector", lambda e: e.tensor_tensor(
            l2v[:], l2v[:], l2r[:], OP.subtract), inc="sv")
        P.op("scalar", lambda e: e.activation(l2r[:], l2v[:], AF.Ln,
                                              bias=1e-5),
             waits=[("sv", v["sv"])])
        s = P.op("scalar", lambda e: e.activation(l2r[:], l2r[:],
                                                  AF.Exp, scale=-0.5),
                 inc="ss")
        P.op("vector", lambda e: e.tensor_tensor(
            l2mr[:], l2m[:], l2r[:], OP.mult), waits=[("ss", s["ss"])])
        P.op("vector", lambda e: e.tensor_tensor(
            x2o[:], r2[:], brd(l2r, DCH, W2), OP.mult))
        v = P.op("vector", lambda e: e.tensor_tensor(
            x2o[:], x2o[:], brd(l2mr, DCH, W2), OP.subtract), inc="sv")
        st = P.op("sync", dma(out_h[b].rearrange("(c p) t -> p c t", p=128)
                              [:, :, t0:t0 + TBF], x2o[:]),
                  waits=[("sv", v["sv"])], inc="so")
        ev["blk_free"] = {"sv": v["sv"], "so": st["so"]}

    for b in range(BPC):
        for j in range(NBA):
            emit_attn_block(b, j)
        emit_ffn_load(b, 0)
        emit_ffn_trees(b, 0)
        for j in range(NBF):
            emit_ffn_block(b, j)

    fin = P.cur("so")
    P.ops.append(("sync", [("so", fin)], lambda e: None, []))

    with nc.Block() as block:
        for name in ("sync", "vector", "scalar", "tensor", "gpsimd"):
            def make(nm):
                def body(eng):
                    P.replay(nm, eng)
                return body
            getattr(block, name)(make(name))

    ctx.close()
    return nc


def _host_layout(inputs):
    f = np.ascontiguousarray
    i = inputs
    shared = {
        "winT": f(i["L0_attn_in_proj_w"].T),
        "wxpT": f(i["L0_attn_x_proj_w"].T),
        "wdtT": f(i["L0_attn_dt_proj_w"].T),
        "woutT": f(i["L0_attn_out_proj_w"].T),
        "wskT": f(i["L0_attn_skip_proj_w"].T),
        "w1T": f(i["L0_ffn_w1"].T),
        "w2T": f(i["L0_ffn_w2"].T),
        "b_in": f(i["L0_attn_in_proj_b"].reshape(16, 128).T),
        "cw1": f(i["L0_attn_conv_w"][:, 0, :].reshape(8, 128, 4)
                 .transpose(1, 0, 2).reshape(128, 32)),
        "cb1": f(i["L0_attn_conv_b"].reshape(8, 128).T),
        "bxp": f(i["L0_attn_x_proj_b"][:, None]),
        "bdt": f(i["L0_attn_dt_proj_b"].reshape(8, 128).T),
        "alog": f(np.repeat(i["L0_attn_A_log"].reshape(1, 16), 128, 0)),
        "dpar": f(i["L0_attn_D_param"].reshape(8, 128).T),
        "b_out": f(i["L0_attn_out_proj_b"].reshape(8, 128).T),
        "b_sk": f(i["L0_attn_skip_proj_b"].reshape(8, 128).T),
        "cw2": f(i["L0_ffn_conv_w"][:, 0, :].reshape(32, 128, 3)
                 .transpose(1, 0, 2).reshape(128, 96)),
        "cb2": f(i["L0_ffn_conv_b"].reshape(32, 128).T),
        "ident": np.eye(128, dtype=np.float32),
    }
    shared = {k: np.asarray(v, np.float32) for k, v in shared.items()}
    x = i["L0_attn_x"]
    in_maps = []
    for c in range(NC):
        m = dict(shared)
        m["xT"] = np.asarray(
            f(x[c * BPC:(c + 1) * BPC].transpose(0, 2, 1)), np.float32)
        in_maps.append(m)
    return in_maps


def kernel(**inputs):
    if "nc" not in _CACHE:
        _CACHE["nc"] = build_program(debug=False)
    nc = _CACHE["nc"]
    in_maps = _host_layout(inputs)
    res = run_bass_kernel_spmd(nc, in_maps, core_ids=list(range(NC)))
    outs = []
    for c in range(NC):
        o = np.asarray(res.results[c]["out"])
        outs.append(o.transpose(0, 2, 1))
    return np.ascontiguousarray(np.concatenate(outs, axis=0))


# version 20 (final)
# speedup vs baseline: 1.0933x; 1.0064x over previous
import contextlib
import numpy as np
import ml_dtypes
import concourse.bass as bass
import concourse.mybir as mybir
from concourse.bass_utils import run_bass_kernel_spmd
from concourse.alu_op_type import AluOpType as OP

F32 = mybir.dt.float32
BF16 = mybir.dt.bfloat16
F16 = mybir.dt.float16
U8 = mybir.dt.uint8
AF = mybir.ActivationFunctionType
AX = mybir.AxisListType
bfnp = ml_dtypes.bfloat16

NC = 8
BPC = 2
T = 2048
D = 1024
DCH = 8
H = 4096
HCH = 32
N16 = 16
TBA = 512
NBA = T // TBA
TBF = 512
NBF = T // TBF
MAGIC = 1536.0

_CACHE = {}


class Prog:

    def __init__(self, nc, sems):
        self.nc = nc
        self.sems = sems
        self.cnt = {k: 0 for k in sems}
        self.ops = []

    def op(self, eng, fn, waits=(), inc=None, amt=None):
        incs = []
        if inc is not None:
            a = amt if amt is not None else (16 if inc in ("ld", "so") else 1)
            self.cnt[inc] += a
            incs.append((inc, a))
        self.ops.append((eng, [w for w in waits if w[1] > 0], fn, incs))
        return dict(self.cnt)

    def cur(self, sem):
        return self.cnt[sem]

    def replay(self, eng_name, eng):
        for e, waits, fn, incs in self.ops:
            if e != eng_name:
                continue
            for s, v in waits:
                eng.wait_ge(self.sems[s], v)
            inst = fn(eng)
            for s, a in incs:
                inst.then_inc(self.sems[s], a)


def build_program(debug=False):
    nc = bass.Bass("TRN2", target_bir_lowering=False,
                   detect_race_conditions=False, debug=False)

    dram = {}

    def din(name, shape, dtype=F32):
        dram[name] = nc.dram_tensor(name, shape, dtype, kind="ExternalInput").ap()
        return dram[name]

    xT = din("xT", [BPC, D, T])
    winT = din("winT", [D, 2 * D])
    wxpT = din("wxpT", [D, 96])
    wdtT = din("wdtT", [64, D])
    woutT = din("woutT", [D, D])
    wskT = din("wskT", [D, D])
    w1T = din("w1T", [D, H])
    w2T = din("w2T", [H, D])
    b_in = din("b_in", [128, 16])
    cw1 = din("cw1", [128, DCH * 4])
    cb1 = din("cb1", [128, DCH])
    bxp = din("bxp", [96, 1])
    bdt = din("bdt", [128, DCH])
    alog = din("alog", [128, N16])
    dpar = din("dpar", [128, DCH])
    b_out = din("b_out", [128, DCH])
    b_sk = din("b_sk", [128, DCH])
    cw2 = din("cw2", [128, HCH * 3])
    cb2 = din("cb2", [128, HCH])
    identh = din("ident", [128, 128])

    out_h = nc.dram_tensor("out", [BPC, D, T], F32, kind="ExternalOutput").ap()
    x1_h = nc.dram_tensor("x1sc", [BPC, D, T], BF16, kind="Internal").ap()
    w1q_h = nc.dram_tensor("w1qsc", [HCH, D, 128], BF16, kind="Internal").ap()
    w2q_h = nc.dram_tensor("w2qsc", [H, D], BF16, kind="Internal").ap()
    dbg_h = {}
    if debug:
        dbg_h["d_xc"] = nc.dram_tensor("d_xc", [D, T], BF16,
                                       kind="ExternalOutput").ap()
        dbg_h["d_delta"] = nc.dram_tensor("d_delta", [D, T], F32,
                                          kind="ExternalOutput").ap()
        dbg_h["d_y"] = nc.dram_tensor("d_y", [D, T], F32,
                                      kind="ExternalOutput").ap()
        dbg_h["d_x1"] = nc.dram_tensor("d_x1", [D, T], BF16,
                                       kind="ExternalOutput").ap()
        dbg_h["d_ssm"] = nc.dram_tensor("d_ssm", [96, T], F32,
                                        kind="ExternalOutput").ap()
        dbg_h["d_res"] = nc.dram_tensor("d_res", [D, T], F32,
                                        kind="ExternalOutput").ap()
        dbg_h["d_xq"] = nc.dram_tensor("d_xq", [D, 520], BF16,
                                       kind="ExternalOutput").ap()
        dbg_h["d_s1"] = nc.dram_tensor("d_s1", [128, 520], F32,
                                       kind="ExternalOutput").ap()
        dbg_h["d_e1"] = nc.dram_tensor("d_e1", [128, 520], F32,
                                       kind="ExternalOutput").ap()
        dbg_h["d_hs"] = nc.dram_tensor("d_hs", [H, 520], BF16,
                                       kind="ExternalOutput").ap()
        dbg_h["d_hsil"] = nc.dram_tensor("d_hsil", [H, TBF], BF16,
                                         kind="ExternalOutput").ap()
        dbg_h["d_btmf"] = nc.dram_tensor("d_btmf", [128, 2, 520], BF16,
                                         kind="ExternalOutput").ap()
        dbg_h["d_x1s1"] = nc.dram_tensor("d_x1s1", [128, 8, 520], BF16,
                                         kind="ExternalOutput").ap()
        dbg_h["d_tmax1"] = nc.dram_tensor("d_tmax1", [128, 520], F32,
                                          kind="ExternalOutput").ap()
        dbg_h["d_s1b1"] = nc.dram_tensor("d_s1b1", [128, 520], F32,
                                         kind="ExternalOutput").ap()
        dbg_h["d_xq1"] = nc.dram_tensor("d_xq1", [D, 520], BF16,
                                        kind="ExternalOutput").ap()
        dbg_h["d_hst"] = nc.dram_tensor("d_hst", [128, HCH * 3], BF16,
                                        kind="ExternalOutput").ap()
        dbg_h["d_r2"] = nc.dram_tensor("d_r2", [D, TBF], F32,
                                       kind="ExternalOutput").ap()

    ctx = contextlib.ExitStack()

    def sbt(name, shape, dtype):
        return ctx.enter_context(nc.sbuf_tensor(name, shape, dtype))

    w_in = sbt("w_in", [128, DCH, 2 * D], BF16)
    w_out = sbt("w_out", [128, DCH, D], BF16)
    w_sk = sbt("w_sk", [128, DCH, D], BF16)
    w_xp = sbt("w_xp", [128, DCH, 96], BF16)
    w_dt = sbt("w_dt", [64, D], BF16)

    p_bin = sbt("p_bin", [128, 16], F32)
    p_cw1 = sbt("p_cw1", [128, DCH * 4], F32)
    p_cb1 = sbt("p_cb1", [128, DCH], F32)
    p_bxp = sbt("p_bxp", [96, 1], F32)
    p_bdt = sbt("p_bdt", [128, DCH], F32)
    p_A = sbt("p_A", [128, N16], F32)
    p_D = sbt("p_D", [128, DCH], F32)
    p_bos = sbt("p_bos", [128, DCH], F32)
    p_bsk = sbt("p_bsk", [128, DCH], F32)
    p_cw2 = sbt("p_cw2", [128, HCH * 3], F32)
    p_cb2 = sbt("p_cb2", [128, HCH], F32)
    ones_bf = sbt("ones_bf", [128, 1], BF16)
    xtail = sbt("xtail", [128, DCH * 3], BF16)
    ident128 = sbt("ident128", [128, 128], F32)
    identbf = sbt("identbf", [128, 128], BF16)
    p_eps = sbt("p_eps", [128, 1], F32)
    nc.const_aps.register(F32, 1e-5, p_eps[:])
    p_half = sbt("p_half", [128, 2], F32)
    nc.const_aps.register(F32, 0.5, p_half[:, 0:1])
    nc.const_aps.register(F32, -0.5, p_half[:, 1:2])
    swv = sbt("swv", [1, 8], F32)
    swrep = sbt("swrep", [128, 4], F32)
    carry = sbt("carry", [128, N16 * DCH], F32)
    ones128 = sbt("ones128", [128, 128], BF16)
    ssm32 = sbt("ssm32", [96, TBA], F32)
    bpcp = sbt("bpcp", [32, TBA], BF16)
    dt_bf = sbt("dt_bf", [64, TBA], BF16)

    AB = 116160
    arena = sbt("arena", [128, AB], U8)

    def av(off, shape, dtype):
        n = int(np.prod(shape)) * mybir.dt.size(dtype)
        assert off % 4 == 0 and off + n <= AB, f"arena {off}+{n}"
        v = arena[:, off:off + n].bitcast(dtype)
        if len(shape) > 1:
            lets = "abcd"[: len(shape)]
            v = v.rearrange(f"p ({' '.join(lets)}) -> p {' '.join(lets)}",
                            **{l: s for l, s in zip(lets, shape)})
        return v

    x32 = av(0, [DCH, TBA], F32)
    x_bf = av(16384, [DCH, TBA], BF16)
    XIW = 520
    x_in = av(24576, [DCH, XIW], BF16)
    silres = av(33024, [DCH, TBA], BF16)
    xc_bf = av(41216, [DCH, TBA], BF16)
    delta = av(49408, [DCH, TBA], F32)
    dA = av(65792, [2, DCH, TBA], BF16)
    y3_bf = av(65792, [DCH, TBA], BF16)
    dBu = av(82176, [DCH, TBA], BF16)
    tmpn = av(98560, [DCH, TBA], BF16)
    x1blk = av(82176, [DCH, TBA], BF16)
    ys = av(90368, [DCH, TBA], BF16)
    y_acc = av(98560, [DCH, TBA], F32)
    ln_t = av(98560, [DCH, TBA], F32)
    scr32 = av(82176, [2, TBA], F32)
    conv_acc = av(90368, [2, TBA], BF16)
    fca = av(99200, [2, TBF], BF16)
    sq_bf = av(24576, [DCH, TBA], BF16)

    X1W = 520
    x1sl = [av(0, [DCH, X1W], BF16),
            av(8320, [DCH, X1W], BF16)]
    btm_f = av(33280, [2, X1W], BF16)
    x2o = av(58240, [DCH, TBF], F32)
    Xq = av(16640, [DCH, X1W], BF16)
    HSW = 520
    hs = av(24960, [HCH, HSW], BF16)
    hsil = av(58240, [HCH, TBF], BF16)
    w1s = av(91008, [2, DCH, 128], BF16)
    w2s = av(95104, [2, DCH, 128], BF16)
    w2c = av(91008, [2, 16, 128], BF16)
    r2 = av(99200, [DCH, TBF], F32)
    r2f = av(99200, [DCH * TBF], F32)
    tqg = av(99200, [4, X1W], F32)
    btm = av(107520, [2, X1W], BF16)
    qz16 = av(107520, [4, X1W], F16)
    qi16 = av(111680, [4, X1W], mybir.dt.int16)
    qscr = av(16640, [3, 512], F32)

    q32bf = av(33280, [520], BF16)
    wqs32 = av(0, [2, 4096], F32)
    wqbf = av(32768, [2, 4096], BF16)
    wq16b = av(49152, [2, 4096], BF16)
    wqprt = av(65536, [40], F32)
    wq2 = av(65792, [40], F32)
    wq2bf = av(66048, [40], BF16)
    sfold = av(66176, [96], F32)
    lnm = av(49408, [TBA], F32)
    lnv = av(51456, [TBA], F32)
    lnr = av(53504, [TBA], F32)
    lnmr = av(55552, [TBA], F32)
    l2m = av(16640, [TBF], F32)
    l2v = av(18688, [TBF], F32)
    l2r = av(20736, [TBF], F32)
    l2mr = av(22784, [TBF], F32)

    BCB = 12544
    bca = sbt("bca", [128, BCB], U8)

    def bv(off, shape, dtype):
        n = int(np.prod(shape)) * mybir.dt.size(dtype)
        assert off + n <= BCB
        v = bca[:, off:off + n].bitcast(dtype)
        if len(shape) > 1:
            lets = "abcd"[: len(shape)]
            v = v.rearrange(f"p ({' '.join(lets)}) -> p {' '.join(lets)}",
                            **{l: s for l, s in zip(lets, shape)})
        return v

    Bp_b = bv(0, [4, TBA], BF16)
    Cp_b = bv(4160, [4, TBA], BF16)
    s1_b = bv(8320, [X1W], F32)
    e1_b = bv(10400, [X1W], F32)
    s2_b = bv(0, [TBF], F32)
    e2_b = bv(2080, [TBF], F32)
    hstash = bv(12480, [HCH], BF16)

    psb = [ctx.enter_context(nc.psum_tensor(f"psb{i}", [128, 512], F32))
           for i in range(8)]
    ps_la = psb[6]

    sems = {k: ctx.enter_context(nc.semaphore(k))
            for k in ("ld", "so", "sv", "ss", "st", "sg")}
    P = Prog(nc, sems)
    ev = {"blk_free": {"sv": 0, "so": 0}, "la_prev": 0}

    def dma(dst, src):
        return lambda e: e.dma_start(out=dst, in_=src)

    def bcast(dst, src_row, width):
        src = src_row.unsqueeze(1).broadcast_to([1, 128, width])
        return lambda e: e.dma_start(out=dst, in_=src)

    def brd(vec, n, w):
        return vec[:, 0:w].unsqueeze(1).broadcast_to([128, n, w])

    w = P.op("sync", dma(ident128[:], identh[:]), inc="ld")
    P.op("vector", lambda e: e.tensor_copy(identbf[:], ident128[:]),
         waits=[("ld", w["ld"])], inc="sv")
    for dst, src in [(p_bin, b_in), (p_cw1, cw1), (p_cb1, cb1), (p_bxp, bxp),
                     (p_bdt, bdt), (p_A, alog), (p_D, dpar), (p_bos, b_out),
                     (p_bsk, b_sk), (p_cw2, cw2), (p_cb2, cb2)]:
        P.op("sync", dma(dst[:], src[:]), inc="ld")
    ld_small = P.cur("ld")

    def w_cast(dst, src_c, nchunks, width):
        cev = {}
        for c in range(nchunks):
            stg = wqs32[:, c % 2, 0:width]
            wts = ([("sv", P.cur("sv"))] if c < 2
                   else [("sv", cev[c - 2])])
            w = P.op("sync", (lambda s_, cc: dma(s_, src_c[cc]))(stg, c),
                     waits=wts, inc="ld")
            v = P.op("vector",
                     (lambda d_, wdt: lambda e: e.tensor_copy(d_, wdt))
                     (dst[:, c, :], stg),
                     waits=[("ld", w["ld"])], inc="sv")
            cev[c] = v["sv"]

    w_cast(w_in, winT.rearrange("(c p) o -> c p o", p=128), DCH, 2 * D)
    w_cast(w_out, woutT.rearrange("(c p) o -> c p o", p=128), DCH, D)
    w_cast(w_sk, wskT.rearrange("(c p) o -> c p o", p=128), DCH, D)
    w_cast(w_xp, wxpT.rearrange("(c p) o -> c p o", p=128), DCH, 96)
    w = P.op("sync", dma(wqs32[0:64, 0, 0:D], wdtT[:]),
             waits=[("sv", P.cur("sv"))], inc="ld")
    P.op("vector", lambda e: e.tensor_copy(w_dt[:], wqs32[0:64, 0, 0:D]),
         waits=[("ld", w["ld"])], inc="sv")

    P.op("vector", lambda e: e.memset(ones_bf[:], 1.0), waits=[("ld", ld_small)])
    P.op("vector", lambda e: e.memset(ones128[:], 1.0))
    P.op("vector", lambda e: e.memset(p_eps[:], 1e-5))
    P.op("vector", lambda e: e.memset(p_half[:, 0:1], 0.5))
    P.op("vector", lambda e: e.memset(p_half[:, 1:2], -0.5))
    P.op("vector", lambda e: e.memset(sfold[:, 64:96], 0.0))
    P.op("vector", lambda e: e.memset(qscr[:, 2, 0:128], 0.0))
    P.op("vector", lambda e: e.tensor_tensor(p_bos[:], p_bos[:], p_bsk[:],
                                             OP.add))
    P.op("scalar", lambda e: e.activation(p_A[:], p_A[:], AF.Exp),
         waits=[("ld", ld_small)], inc="ss")
    P.op("vector", lambda e: e.tensor_scalar(p_A[:], p_A[:], -1.0, None,
                                             OP.mult),
         waits=[("ss", P.cur("ss"))], inc="sv")

    w1rows = w1T.rearrange("(c p) h -> c p h", p=128)
    w2rows = w2T.rearrange("(c p) dd -> c p dd", p=128)

    def qpass1(rows, nchunks, width, prt0):
        rev = {}
        for c in range(nchunks):
            stg = wqs32[:, c % 2, 0:width]
            wts = ([("sv", P.cur("sv"))] if c < 2
                   else [("sv", rev[c - 2])])
            w = P.op("sync", (lambda s_, cc: dma(s_, rows[cc]))(stg, c),
                     waits=wts, inc="ld")
            P.op("vector", (lambda s_, cc: lambda e: e.tensor_reduce(
                wqprt[:, prt0 + cc:prt0 + cc + 1], s_, AX.X,
                OP.add, apply_absolute_value=True))(stg, c),
                waits=[("ld", w["ld"])])
            v = P.op("vector",
                     lambda e: e.tensor_copy(sfold[:, 64:80],
                                             sfold[:, 80:96]), inc="sv")
            rev[c] = v["sv"]

    qpass1(w1rows, DCH, H, 0)
    qpass1(w2rows, HCH, D, 8)
    w = P.op("sync", dma(wq2[:, 0:40], wqprt[:, 0:40]),
             waits=[("sv", P.cur("sv"))], inc="ld")
    v = P.op("vector", lambda e: e.tensor_copy(wq2bf[:, 0:40], wq2[:, 0:40]),
             waits=[("ld", w["ld"])], inc="sv")
    t = P.op("tensor", lambda e: e.matmul(psb[4][0:1, 0:40], ones_bf[:],
                                          wq2bf[:, 0:40], start=True,
                                          stop=True),
             waits=[("sv", v["sv"])], inc="st")

    def sspacer(e):
        return e.tensor_copy(sfold[:, 64:80], sfold[:, 80:96])

    f = sfold[0:1, 0:64]
    P.op("vector", lambda e: e.tensor_copy(f[:, 16:56], psb[4][0:1, 0:40]),
         waits=[("st", t["st"])])
    P.op("vector", sspacer)
    P.op("vector", lambda e: e.tensor_tensor(f[:, 0:4], f[:, 16:20],
                                             f[:, 20:24], OP.add))
    P.op("vector", sspacer)
    P.op("vector", lambda e: e.tensor_tensor(f[:, 4:6], f[:, 0:2], f[:, 2:4],
                                             OP.add))
    P.op("vector", sspacer)
    P.op("vector", lambda e: e.tensor_tensor(swv[:, 0:1], f[:, 4:5],
                                             f[:, 5:6], OP.add))
    P.op("vector", sspacer)
    P.op("vector", lambda e: e.tensor_tensor(f[:, 8:24], f[:, 24:40],
                                             f[:, 40:56], OP.add))
    P.op("vector", sspacer)
    P.op("vector", lambda e: e.tensor_tensor(f[:, 24:32], f[:, 8:16],
                                             f[:, 16:24], OP.add))
    P.op("vector", sspacer)
    P.op("vector", lambda e: e.tensor_tensor(f[:, 0:4], f[:, 24:28],
                                             f[:, 28:32], OP.add))
    P.op("vector", sspacer)
    P.op("vector", lambda e: e.tensor_tensor(f[:, 4:6], f[:, 0:2], f[:, 2:4],
                                             OP.add))
    P.op("vector", sspacer)
    P.op("vector", lambda e: e.tensor_tensor(swv[:, 1:2], f[:, 4:5],
                                             f[:, 5:6], OP.add))
    P.op("vector", sspacer)
    P.op("vector", lambda e: e.tensor_scalar(swv[:, 0:2], swv[:, 0:2],
                                             1.0 / (D * H), 1e-5, OP.mult,
                                             OP.max))
    P.op("vector", sspacer)
    P.op("vector", lambda e: e.reciprocal(swv[:, 2:4], swv[:, 0:2]))
    P.op("vector", sspacer)
    sw = P.op("vector", lambda e: e.tensor_scalar(swv[:, 4:6], swv[:, 0:2],
                                                  1.0 / 254, None, OP.mult),
              inc="sv")
    bw = P.op("sync", bcast(swrep[:, 0:4], swv[0:1, 2:6], 4),
              waits=[("sv", sw["sv"])], inc="ld")
    sw_ld = bw["ld"]

    def qpass2(rows, dst_fn, nchunks, width, sw_col):
        sig_ev = {}
        st_ev = {}
        for c in range(nchunks):
            stg = wqs32[:, c % 2, 0:width]
            bfb = wqbf[:, c % 2, 0:width]
            s16 = wq16b[:, c % 2, 0:width]
            wts = [("ld", sw_ld)]
            if c < 2:
                wts += [("sv", P.cur("sv")), ("so", P.cur("so")),
                        ("ss", P.cur("ss"))]
            else:
                wts.append(("ss", sig_ev[c - 2]))
            w = P.op("sync", (lambda s_, cc: dma(s_, rows[cc]))(stg, c),
                     waits=wts, inc="ld")
            v = P.op("vector", (lambda s_: lambda e: e.tensor_scalar(
                s_, s_, swrep[:, sw_col:sw_col + 1], None, OP.mult))(stg),
                waits=[("ld", w["ld"])], inc="sv")
            swts = [("sv", v["sv"])]
            swts.append(("so", st_ev[c - 2]) if c >= 2
                        else ("so", P.cur("so")))
            s = P.op("scalar", (lambda b_, s_: lambda e: e.sign(
                b_, s_, bias=-0.5))(bfb, stg), waits=swts)
            s = P.op("scalar", (lambda b_, s_: lambda e: e.sign(
                b_, s_, bias=0.5))(s16, stg), inc="ss")
            sig_ev[c] = s["ss"]
            v = P.op("vector", (lambda b_, s_: lambda e: e.tensor_tensor(
                b_, b_, s_, OP.add))(bfb, s16),
                waits=[("ss", s["ss"])], inc="sv")
            st = P.op("gpsimd", (lambda b_, cc: dma(dst_fn(cc), b_))(bfb, c),
                      waits=[("sv", v["sv"])], inc="so")
            st_ev[c] = st["so"]

    qpass2(w1rows, lambda c: w1q_h[:, c * 128:(c + 1) * 128, :]
           .transpose([1, 0, 2]), DCH, H, 0)
    qpass2(w2rows, lambda c: w2q_h.rearrange("(c p) dd -> c p dd", p=128)[c],
           HCH, D, 1)
    ev["wq_done"] = P.cur("so")
    ev["blk_free"] = {"sv": P.cur("sv"), "so": P.cur("so")}


    def emit_quant_scales(tmax, W, blkw, swcol, s_dst, e_dst, fold_ev):
        npc = blkw // 128

        def spacer(e):
            return e.tensor_copy(qscr[:, 2, 0:64], qscr[:, 2, 64:128])

        for pc in range(npc):
            g = P.op("tensor", (lambda p_: lambda e: e.transpose(
                psb[4][:, p_ * 128:(p_ + 1) * 128],
                tmax[:, p_ * 128:(p_ + 1) * 128], ident128[:]))(pc),
                waits=[("sv", fold_ev)] if pc == 0 else (),
                inc="st" if (pc == npc - 1 and W == blkw) else None)
        if W > blkw:
            g = P.op("tensor", lambda e: e.transpose(
                psb[5][0:1, 0:128], tmax[:, blkw:blkw + 1], ident128[:]),
                inc="st")
        b0 = qscr[:, 0, 0:512].rearrange("p (a b) -> p a b", a=npc)
        P.op("vector", lambda e: e.tensor_copy(
            b0, psb[4][:, 0:npc * 128].rearrange("p (a b) -> p a b", a=npc)),
            waits=[("st", g["st"])])
        P.op("vector", spacer)
        tA = qscr[:, 1, 0:npc * 64].rearrange("p (a b) -> p a b", a=npc)
        tB = qscr[:, 1, 256:256 + npc * 32].rearrange("p (a b) -> p a b", a=npc)
        tC = qscr[:, 1, 384:384 + npc * 16].rearrange("p (a b) -> p a b", a=npc)
        t4 = qscr[:, 1, 448:448 + npc * 8].rearrange("p (a b) -> p a b", a=npc)
        t5 = qscr[:, 1, 480:480 + npc * 4].rearrange("p (a b) -> p a b", a=npc)
        t6 = qscr[:, 1, 496:496 + npc * 2].rearrange("p (a b) -> p a b", a=npc)
        steps = [(tA, b0, 64), (tB, tA, 32), (tC, tB, 16), (t4, tC, 8),
                 (t5, t4, 4), (t6, t5, 2)]
        for out_v, in_v, w_ in steps:
            P.op("vector", (lambda o_, i_, ww: lambda e: e.tensor_tensor(
                o_, i_[:, :, 0:ww], i_[:, :, ww:2 * ww], OP.max))
                (out_v, in_v, w_))
            P.op("vector", spacer)
        mxq = qscr[:, 1, 504:504 + npc]
        P.op("vector", lambda e: e.tensor_tensor(
            mxq.rearrange("p (a b) -> p a b", a=npc), t6[:, :, 0:1],
            t6[:, :, 1:2], OP.max))
        P.op("vector", spacer)
        mxc = qscr[:, 1, 508:508 + npc]
        sev_v = qscr[:, 0, 0:2 * npc]
        P.op("vector", lambda e: e.tensor_scalar(mxc, mxq, 1e-5, None, OP.max))
        P.op("vector", spacer)
        P.op("vector", lambda e: e.reciprocal(sev_v[:, 0:npc], mxc))
        P.op("vector", spacer)
        P.op("vector", lambda e: e.tensor_scalar(sev_v[:, 0:npc],
                                                 sev_v[:, 0:npc], 127.0, None,
                                                 OP.mult))
        P.op("vector", spacer)
        v = P.op("vector", lambda e: e.tensor_scalar(
            sev_v[:, npc:2 * npc], mxc, swrep[:, 2 + swcol:3 + swcol], None,
            OP.mult), inc="sv")
        g = P.op("tensor", lambda e: e.transpose(
            psb[5][0:2 * npc, 128:256], sev_v, ident128[:]),
            waits=[("sv", v["sv"])], inc="st")
        v = P.op("vector", lambda e: e.tensor_copy(
            qscr[0:2 * npc, 0, 16:144], psb[5][0:2 * npc, 128:256]),
            waits=[("st", g["st"])], inc="sv")
        sev = v["sv"]
        if W > blkw:
            c = qscr[0:1, 0, 160:352]
            P.op("vector", lambda e: e.tensor_copy(c[:, 0:128],
                                                   psb[5][0:1, 0:128]))
            P.op("vector", spacer)
            for (o0, o1, i0, ww) in [(128, 192, 0, 64), (0, 32, 128, 32),
                                     (32, 48, 0, 16), (48, 56, 32, 8),
                                     (56, 60, 48, 4), (60, 62, 56, 2),
                                     (62, 63, 60, 1)]:
                P.op("vector", (lambda a_, b_, c_, ww_: lambda e:
                     e.tensor_tensor(c[:, a_:b_], c[:, c_:c_ + ww_],
                                     c[:, c_ + ww_:c_ + 2 * ww_], OP.max))
                     (o0, o1, i0, ww))
                P.op("vector", spacer)
            P.op("vector", lambda e: e.tensor_scalar(c[:, 64:65], c[:, 62:63],
                                                     1e-5, None, OP.max))
            P.op("vector", spacer)
            P.op("vector", lambda e: e.reciprocal(c[:, 65:66], c[:, 64:65]))
            P.op("vector", spacer)
            P.op("vector", lambda e: e.tensor_scalar(c[:, 65:66], c[:, 65:66],
                                                     127.0, None, OP.mult))
            P.op("vector", spacer)
            v = P.op("vector", (lambda sc: lambda e: e.tensor_scalar(
                c[:, 66:67], c[:, 64:65], swv[0:1, 4 + sc:5 + sc], None,
                OP.mult))(swcol), inc="sv")
            sev = v["sv"]
        for pc in range(npc):
            P.op("sync", (lambda p_: bcast(s_dst[:, p_ * 128:(p_ + 1) * 128],
                                           qscr[p_:p_ + 1, 0, 16:144], 128))
                 (pc), waits=[("sv", sev)] if pc == 0 else (), inc="ld")
            w = P.op("sync", (lambda p_: bcast(
                e_dst[:, p_ * 128:(p_ + 1) * 128],
                qscr[npc + p_:npc + p_ + 1, 0, 16:144], 128))(pc), inc="ld")
        if W > blkw:
            P.op("sync", bcast(s_dst[:, blkw:blkw + 1],
                               qscr[0:1, 0, 225:226], 1), inc="ld")
            w = P.op("sync", bcast(e_dst[:, blkw:blkw + 1],
                                   qscr[0:1, 0, 226:227], 1), inc="ld")
        return w["ld"]

    def emit_attn_block(b, j):
        t0 = j * TBA
        f = ev["blk_free"]
        w = P.op("sync", dma(x32[:], xT[b].rearrange("(c p) t -> p c t", p=128)
                             [:, :, t0:t0 + TBA]),
                 waits=[("sv", f["sv"]), ("so", f["so"])], inc="ld")
        x_ld = w["ld"]
        if j == 0:
            h0 = P.op("vector", lambda e: e.memset(x_in[:, :, 1:4], 0.0),
                      inc="sv")
        else:
            h0 = P.op("vector", lambda e: e.tensor_copy(
                x_in[:, :, 1:4],
                xtail[:].rearrange("p (a b) -> p a b", a=DCH)), inc="sv")
        halo_ev = h0["sv"]
        xbf_ev = {}
        for c in range(DCH):
            v = P.op("vector", (lambda cc: lambda e: e.tensor_copy(
                x_bf[:, cc, :], x32[:, cc, :]))(c),
                waits=[("ld", x_ld)] if c == 0 else (), inc="sv")
            xbf_ev[c] = v["sv"]

        evac_ev = {}
        for o in range(16):
            bank = psb[o % 3]
            for k in range(DCH):
                wts = []
                if o == 0:
                    wts.append(("sv", xbf_ev[k]))
                elif k == 0 and o >= 3:
                    wts.append(("ss", evac_ev[o - 3]))
                g = P.op("tensor", (lambda bk, oo, kk: lambda e: e.matmul(
                    bk[:], w_in[:, kk, oo * 128:(oo + 1) * 128], x_bf[:, kk, :],
                    start=(kk == 0), stop=(kk == DCH - 1)))(bank, o, k),
                    waits=wts,
                    inc="st" if k == DCH - 1 else None)
            if o < 8:
                s = P.op("scalar", (lambda bk, oo: lambda e: e.activation(
                    x_in[:, oo, 4:516], bk[:], AF.Identity,
                    bias=p_bin[:, oo:oo + 1]))(bank, o),
                    waits=[("st", g["st"]), ("sv", halo_ev)], inc="ss")
            else:
                s = P.op("scalar", (lambda bk, oo: lambda e: e.activation(
                    silres[:, oo - 8, :], bk[:], AF.Silu,
                    bias=p_bin[:, oo:oo + 1]))(bank, o),
                    waits=[("st", g["st"])], inc="ss")
            evac_ev[o] = s["ss"]

        xc_ev = {}
        v3_ev = {}
        for c in range(DCH):
            ca = conv_acc[:, c % 2, :]
            wts = [("ss", evac_ev[c])]
            if c >= 2:
                wts.append(("ss", xc_ev[c - 2]))
            P.op("vector", (lambda cc, ca_: lambda e: e.tensor_scalar(
                ca_, x_in[:, cc, 1:513],
                p_cw1[:, cc * 4:cc * 4 + 1], None, OP.mult))(c, ca),
                waits=wts)
            P.op("vector", (lambda cc, ca_: lambda e: e.scalar_tensor_tensor(
                ca_, x_in[:, cc, 2:514],
                p_cw1[:, cc * 4 + 1:cc * 4 + 2], ca_,
                OP.mult, OP.add))(c, ca))
            P.op("vector", (lambda cc, ca_: lambda e: e.scalar_tensor_tensor(
                ca_, x_in[:, cc, 3:515],
                p_cw1[:, cc * 4 + 2:cc * 4 + 3], ca_,
                OP.mult, OP.add))(c, ca))
            v = P.op("vector", (lambda cc, ca_: lambda e:
                     e.scalar_tensor_tensor(
                         ca_, x_in[:, cc, 4:516],
                         p_cw1[:, cc * 4 + 3:cc * 4 + 4], ca_,
                         OP.mult, OP.add))(c, ca), inc="sv")
            v3_ev[c] = v["sv"]
            s = P.op("scalar", (lambda cc, ca_: lambda e: e.activation(
                xc_bf[:, cc, :], ca_, AF.Silu,
                bias=p_cb1[:, cc:cc + 1]))(c, ca),
                waits=[("sv", v["sv"])], inc="ss")
            xc_ev[c] = s["ss"]

        for k in range(DCH):
            g = P.op("tensor", (lambda kk: lambda e: e.matmul(
                psb[3][0:96, :], w_xp[:, kk, :], xc_bf[:, kk, :],
                start=(kk == 0), stop=(kk == DCH - 1)))(k),
                waits=[("ss", xc_ev[k])],
                inc="st" if k == DCH - 1 else None)
        s = P.op("scalar", lambda e: e.activation(ssm32[:], psb[3][0:96, :],
                                                  AF.Identity, bias=p_bxp[:, 0:1]),
                 waits=[("st", g["st"])], inc="ss")
        P.op("vector", lambda e: e.tensor_copy(bpcp[:], ssm32[64:96, :]),
             waits=[("ss", s["ss"]), ("ld", ev.get("bc_last", 0))])
        v = P.op("vector", lambda e: e.tensor_copy(dt_bf[:], ssm32[0:64, :]),
                 inc="sv")
        ssmc_ev = v["sv"]

        dsp_ev = {}
        for o in range(DCH):
            bank = psb[o % 3]
            wts = [("sv", ssmc_ev)]
            if o >= 3:
                wts.append(("ss", dsp_ev[o - 3]))
            g = P.op("tensor", (lambda bk, oo: lambda e: e.matmul(
                bk[:], w_dt[:, oo * 128:(oo + 1) * 128], dt_bf[:],
                start=True, stop=True))(bank, o), waits=wts, inc="st")
            P.op("scalar", (lambda bk, oo: lambda e: e.activation(
                scr32[:, 0, :], bk[:], AF.Exp, bias=p_bdt[:, oo:oo + 1]))
                (bank, o), waits=[("st", g["st"])])
            s = P.op("scalar", (lambda oo: lambda e: e.activation(
                delta[:, oo, :], scr32[:, 0, :], AF.Ln, bias=1.0))(o),
                inc="ss")
            dsp_ev[o] = s["ss"]

        P.op("vector", lambda e: e.tensor_copy(
            xtail[:].rearrange("p (a b) -> p a b", a=DCH),
            x_in[:, :, 513:516]), waits=[("ss", xc_ev[DCH - 1])])
        u_view = x_in[:, :, 4:516]
        v = P.op("vector", lambda e: e.tensor_tensor(u_view, delta[:], xc_bf[:],
                                                     OP.mult),
                 waits=[("ss", dsp_ev[DCH - 1])], inc="sv")
        u_ev = v["sv"]

        if j == 0:
            P.op("vector", lambda e: e.memset(carry[:], 0.0))
        scan_ev = {}
        gadd_ev = {}
        for n in range(N16):
            h = n % 2
            h4 = n % 4
            wts = [("sv", ssmc_ev)]
            if n >= 4:
                wts.append(("sv", scan_ev[n - 4]))
            P.op("sync", bcast(Bp_b[:, h4, :], bpcp[n:n + 1, :], TBA),
                 waits=wts, inc="ld")
            w = P.op("sync", bcast(Cp_b[:, h4, :], bpcp[16 + n:17 + n, :], TBA),
                     inc="ld")
            bc_ld = w["ld"]
            ev["bc_last"] = bc_ld
            wts = []
            if n >= 2:
                wts.append(("sv", scan_ev[n - 2]))
            s = P.op("scalar", (lambda nn, hh: lambda e: e.activation(
                dA[:, hh], delta[:], AF.Exp, scale=p_A[:, nn:nn + 1]))(n, h),
                waits=wts, inc="ss")
            dA_ev = s["ss"]
            wts = [("ld", bc_ld), ("sv", u_ev)]
            P.op("vector", (lambda hh: lambda e: e.tensor_tensor(
                dBu[:], u_view,
                Bp_b[:, hh, :].unsqueeze(1).broadcast_to([128, DCH, TBA]),
                OP.mult))(h4), waits=wts)
            for c in range(DCH):
                P.op("vector", (lambda nn, hh, cc: lambda e:
                     e.tensor_tensor_scan(
                         ys[:, cc, :], dA[:, hh, cc, :], dBu[:, cc, :],
                         carry[:, nn * DCH + cc:nn * DCH + cc + 1],
                         OP.mult, OP.add))(n, h, c),
                    waits=[("ss", dA_ev)] if c == 0 else ())
            v = P.op("vector", (lambda hh: lambda e: e.tensor_tensor(
                tmpn[:], ys[:],
                Cp_b[:, hh, :].unsqueeze(1).broadcast_to([128, DCH, TBA]),
                OP.mult))(h4),
                waits=[("st", gadd_ev[n - 1])] if n >= 1 else (), inc="sv")
            tmp_ev = v["sv"]
            for c in range(DCH):
                g = P.op("tensor", (lambda nn, cc: lambda e: e.matmul(
                    psb[cc][:], identbf[:], tmpn[:, cc, :],
                    start=(nn == 0), stop=(nn == N16 - 1),
                    skip_group_check=True))(n, c),
                    waits=[("sv", tmp_ev)] if c == 0 else (),
                    inc="st" if c == DCH - 1 else None)
            gadd_ev[n] = g["st"]
            v = P.op("vector", (lambda nn: lambda e: e.tensor_copy(
                carry[:, nn * DCH:(nn + 1) * DCH], ys[:, :, TBA - 1]))(n),
                inc="sv")
            scan_ev[n] = v["sv"]

        for c in range(DCH):
            P.op("vector", (lambda cc: lambda e: e.scalar_tensor_tensor(
                y_acc[:, cc, :], xc_bf[:, cc, :], p_D[:, cc:cc + 1],
                psb[cc][:], OP.mult, OP.add))(c),
                waits=[("st", gadd_ev[N16 - 1])] if c == 0 else ())
        if debug:
            vd = P.op("vector", lambda e: e.tensor_copy(y_acc[:, 0, 0:1],
                                                        y_acc[:, 0, 0:1]),
                      inc="sv")
            P.op("sync", dma(dbg_h["d_y"].rearrange("(c p) t -> p c t", p=128)
                             [:, :, t0:t0 + TBA], y_acc[:]),
                 waits=[("sv", vd["sv"])], inc="so")
        v = P.op("vector", lambda e: e.tensor_tensor(y3_bf[:], y_acc[:],
                                                     silres[:], OP.mult),
                 inc="sv")
        y3_ev = v["sv"]

        rres_ev = {}
        for o in range(DCH):
            bank = psb[o % 3]
            wts = [("sv", y3_ev)]
            if o >= 3:
                wts.append(("sv", rres_ev[o - 3]))
            for k in range(DCH):
                P.op("tensor", (lambda bk, oo, kk: lambda e: e.matmul(
                    bk[:], w_out[:, kk, oo * 128:(oo + 1) * 128],
                    y3_bf[:, kk, :], start=(kk == 0), stop=False))(bank, o, k),
                    waits=wts if k == 0 else ())
            for k in range(DCH):
                g = P.op("tensor", (lambda bk, oo, kk: lambda e: e.matmul(
                    bk[:], w_sk[:, kk, oo * 128:(oo + 1) * 128], x_bf[:, kk, :],
                    start=False, stop=(kk == DCH - 1)))(bank, o, k),
                    inc="st" if k == DCH - 1 else None)
            v = P.op("vector", (lambda bk, oo: lambda e:
                     e.scalar_tensor_tensor(
                         x32[:, oo, :], bk[:], p_bos[:, oo:oo + 1],
                         x32[:, oo, :], OP.add, OP.add))(bank, o),
                 waits=[("st", g["st"])], inc="sv")
            rres_ev[o] = v["sv"]
        r_ev = rres_ev[DCH - 1]
        if debug:
            P.op("sync", dma(dbg_h["d_res"].rearrange("(c p) t -> p c t", p=128)
                             [:, :, t0:t0 + TBA], x32[:]),
                 waits=[("sv", r_ev)], inc="so")
            P.op("sync", dma(dbg_h["d_xc"].rearrange("(c p) t -> p c t", p=128)
                             [:, :, t0:t0 + TBA], xc_bf[:]),
                 waits=[("sv", r_ev)], inc="so")
            P.op("sync", dma(dbg_h["d_delta"]
                             .rearrange("(c p) t -> p c t", p=128)
                             [:, :, t0:t0 + TBA], delta[:]),
                 waits=[("sv", r_ev)], inc="so")
            P.op("sync", dma(dbg_h["d_ssm"][:, t0:t0 + TBA], ssm32[:]),
                 waits=[("sv", r_ev)], inc="so")

        s = P.op("scalar", lambda e: e.activation(sq_bf[:, :, 0:TBA], x32[:],
                                                  AF.Square),
                 waits=[("sv", r_ev)], inc="ss")
        sq_ev = s["ss"]
        v = P.op("vector", lambda e: e.tensor_copy(x_bf[:], x32[:]),
                 waits=[("sv", r_ev)], inc="sv")
        rbf_ev = v["sv"]
        for k in range(DCH):
            P.op("tensor", (lambda kk: lambda e: e.matmul(
                psb[4][:], ones128[:], x_bf[:, kk, :],
                start=(kk == 0), stop=(kk == DCH - 1)))(k),
                waits=[("sv", rbf_ev)] if k == 0 else ())
        for k in range(DCH):
            g = P.op("tensor", (lambda kk: lambda e: e.matmul(
                psb[5][:], ones128[:], sq_bf[:, kk, 0:TBA],
                start=(kk == 0), stop=(kk == DCH - 1)))(k),
                waits=[("ss", sq_ev)] if k == 0 else (),
                inc="st" if k == DCH - 1 else None)
        P.op("vector", lambda e: e.tensor_scalar(lnm[:], psb[4][:],
                                                 1.0 / D, None, OP.mult),
             waits=[("st", g["st"])])
        P.op("vector", lambda e: e.tensor_scalar(lnv[:], psb[5][:],
                                                 1.0 / D, None, OP.mult))
        P.op("vector", lambda e: e.tensor_tensor(lnr[:], lnm[:], lnm[:],
                                                 OP.mult))
        v = P.op("vector", lambda e: e.tensor_tensor(
            lnv[:], lnv[:], lnr[:], OP.subtract), inc="sv")
        P.op("scalar", lambda e: e.activation(lnr[:], lnv[:], AF.Ln,
                                              bias=1e-5),
             waits=[("sv", v["sv"])])
        s = P.op("scalar", lambda e: e.activation(lnr[:], lnr[:],
                                                  AF.Exp, scale=-0.5),
                 inc="ss")
        P.op("vector", lambda e: e.tensor_tensor(
            lnmr[:], lnm[:], lnr[:], OP.mult), waits=[("ss", s["ss"])])
        P.op("vector", lambda e: e.tensor_tensor(
            ln_t[:], x32[:], brd(lnr, DCH, TBA), OP.mult))
        v = P.op("vector", lambda e: e.tensor_tensor(
            x1blk[:], ln_t[:], brd(lnmr, DCH, TBA), OP.subtract), inc="sv")
        x1_ev = v["sv"]
        st = P.op("gpsimd", dma(x1_h[b].rearrange("(c p) t -> p c t", p=128)
                                [:, :, t0:t0 + TBA], x1blk[:]),
                  waits=[("sv", x1_ev)], inc="so")
        if debug:
            P.op("sync", dma(dbg_h["d_x1"].rearrange("(c p) t -> p c t", p=128)
                             [:, :, t0:t0 + TBA], x1blk[:]),
                 waits=[("sv", x1_ev)], inc="so")
        ev["blk_free"] = {"sv": x1_ev, "so": st["so"]}
        ev[f"x1_{b}"] = st["so"]

    def emit_ffn_load(b, j):
        t0 = j * TBF
        lastb = (j == NBF - 1)
        ncols = 512 if lastb else 513
        x1s = x1sl[j % 2]
        w = P.op("sync", (lambda x_, tt, nn: dma(
            x_[:, :, 0:nn], x1_h[b].rearrange("(c p) t -> p c t", p=128)
            [:, :, tt:tt + nn]))(x1s, t0, ncols),
            waits=[("so", ev[f"x1_{b}"]),
                   ("sv", ev.get(f"x1f_free{j % 2}", 0))], inc="ld")
        ev["x1f_ld"] = w["ld"]

    def emit_ffn_trees(b, j):
        lastb = (j == NBF - 1)
        W = 513
        x1s = x1sl[j % 2]
        x1_ld = ev["x1f_ld"]
        if j > 0:
            sv0 = P.op("vector", lambda e: e.tensor_copy(
                hstash[:], hs[:, :, 512:513].rearrange("p a b -> p (a b)")),
                inc="sv")
            if debug and b == 0 and j == 1:
                P.op("sync", dma(dbg_h["d_hst"][:, 0:HCH], hstash[:]),
                     waits=[("sv", sv0["sv"])], inc="so")
        if lastb:
            P.op("vector", (lambda x_: lambda e: e.memset(
                x_[:, :, 512:513], 0.0))(x1s), waits=[("ld", x1_ld)])
        tmaxb = btm_f[:, 0, 0:W]
        tminb = btm_f[:, 1, 0:W]
        P.op("vector", (lambda x_: lambda e: e.tensor_tensor(
            tmaxb, x_[:, 0, 0:W], x_[:, 1, 0:W], OP.max))(x1s),
            waits=[("ld", x1_ld)])
        P.op("vector", (lambda x_: lambda e: e.tensor_tensor(
            tminb, x_[:, 0, 0:W], x_[:, 1, 0:W], OP.min))(x1s))
        for c in range(2, DCH):
            P.op("vector", (lambda x_, cc: lambda e: e.tensor_tensor(
                tmaxb, tmaxb, x_[:, cc, 0:W], OP.max))(x1s, c))
            P.op("vector", (lambda x_, cc: lambda e: e.tensor_tensor(
                tminb, tminb, x_[:, cc, 0:W], OP.min))(x1s, c))
        P.op("vector", lambda e: e.tensor_scalar(tminb, tminb, -1.0, None,
                                                 OP.mult))
        v = P.op("vector", lambda e: e.tensor_tensor(tmaxb, tmaxb, tminb,
                                                     OP.max), inc="sv")
        ev["x1f_tr"] = v["sv"]
        if debug and b == 0 and j == 1:
            P.op("sync", dma(dbg_h["d_btmf"][:], btm_f[:]),
                 waits=[("sv", v["sv"])], inc="so")
            P.op("sync", (lambda x_: dma(dbg_h["d_x1s1"][:], x_[:]))(x1s),
                 waits=[("sv", v["sv"])], inc="so")

    def emit_ffn_block(b, j):
        t0 = j * TBF
        lastb = (j == NBF - 1)
        W = 513
        f = ev["blk_free"]
        x1s = x1sl[j % 2]

        tmax = r2f[:, 0:W]
        v = P.op("vector", lambda e: e.tensor_copy(tmax, btm_f[:, 0, 0:W]),
                 waits=[("sv", ev["x1f_tr"]), ("ld", ev.get("stats_ld", 0)),
                        ("so", f["so"])], inc="sv")
        if debug and b == 0 and j == 1:
            P.op("sync", dma(dbg_h["d_tmax1"][:, 0:W], tmax),
                 waits=[("sv", v["sv"])], inc="so")
        sb_ld = emit_quant_scales(tmax, W, 512, 0, s1_b, e1_b, v["sv"])
        if debug and b == 0 and j == 1:
            P.op("sync", dma(dbg_h["d_s1b1"][:], s1_b[:]),
                 waits=[("ld", sb_ld)], inc="so")
        ev["stats_ld"] = sb_ld
        for g4 in range(2):
            sl = slice(g4 * 4, g4 * 4 + 4)
            P.op("vector", (lambda sl_: lambda e: e.tensor_tensor(
                tqg[:, :, 0:W], x1s[:, sl_, 0:W], brd(s1_b, 4, W), OP.mult))
                (sl), waits=[("ld", sb_ld)])
            P.op("vector", lambda e: e.tensor_scalar(
                qz16[:, :, 0:W], tqg[:, :, 0:W], MAGIC, None, OP.add))
            P.op("vector", (lambda sl_: lambda e: e.tensor_scalar(
                Xq[:, sl_, 0:W], qz16[:, :, 0:W], -MAGIC, None, OP.add))(sl))
        v = P.op("vector", lambda e: e.tensor_tensor(
            Xq[:, :, 0:W], Xq[:, :, 0:W], brd(e1_b, DCH, W), OP.mult),
            inc="sv")
        xq_ev = v["sv"]
        if debug and b == 0 and j == 1:
            P.op("sync", dma(dbg_h["d_xq1"].rearrange("(c p) t -> p c t",
                                                      p=128), Xq[:]),
                 waits=[("sv", xq_ev)], inc="so")
        if debug and b == 0 and j == 0:
            P.op("sync", dma(dbg_h["d_xq"].rearrange("(c p) t -> p c t", p=128),
                             Xq[:]), waits=[("sv", xq_ev)], inc="so")
            P.op("sync", dma(dbg_h["d_s1"][:], s1_b[:]),
                 waits=[("sv", xq_ev)], inc="so")
            P.op("sync", dma(dbg_h["d_e1"][:], e1_b[:]),
                 waits=[("sv", xq_ev)], inc="so")

        if j == 0:
            hh0 = P.op("vector", lambda e: e.memset(hs[:, :, 0:1], 0.0),
                       inc="sv")
        else:
            hh0 = P.op("vector", lambda e: e.tensor_copy(
                hs[:, :, 0:1],
                hstash[:].rearrange("p (a b) -> p a b", b=1)), inc="sv")
            if debug and b == 0 and j == 1:
                hcp = P.op("vector", lambda e: e.tensor_copy(
                    hstash[:], hs[:, :, 0:1].rearrange("p a b -> p (a b)")),
                    waits=[("sv", hh0["sv"])], inc="sv")
                P.op("sync", dma(dbg_h["d_hst"][:, 2 * HCH:3 * HCH],
                                 hstash[:]),
                     waits=[("sv", hcp["sv"])], inc="so")
        hhalo_ev = hh0["sv"]
        w1mm_ev = {}
        hs_ev = {}
        mev = {}
        v3f_ev = {}
        hsil_ev = {}

        def emit_fsilu(hh):
            s = P.op("scalar", (lambda h_: lambda e: e.activation(
                hsil[:, h_, :], fca[:, h_ % 2, :], AF.Silu,
                bias=p_cb2[:, h_:h_ + 1]))(hh),
                waits=[("sv", v3f_ev[hh])], inc="ss")
            hsil_ev[hh] = s["ss"]

        for h in range(HCH):
            wts = [("so", ev["wq_done"])]
            if h >= 2:
                wts.append(("st", w1mm_ev[h - 2]))
            else:
                wts.append(("st", ev.get("mm2_prev", 0)))
            w = P.op("sync", (lambda hh: dma(
                w1s[:, hh % 2], w1q_h[hh].rearrange("(c p) o -> p c o",
                                                    p=128)))(h),
                waits=wts, inc="ld")
            w1_ld = w["ld"]
            bank = psb[h % 3]
            wts = [("sv", xq_ev), ("ld", w1_ld)]
            if h >= 3:
                wts.append(("sv", mev[h - 3]))
            for k in range(DCH):
                P.op("tensor", (lambda bk, hh, kk: lambda e: e.matmul(
                    bk[:], w1s[:, hh % 2, kk, :], Xq[:, kk, 0:512],
                    start=(kk == 0), stop=(kk == DCH - 1)))(bank, h, k),
                    waits=wts if k == 0 else ())
            for k in range(DCH):
                lwts = []
                if h == 0 and k == 0:
                    lwts = [("ss", ev["la_prev"])]
                g = P.op("tensor", (lambda hh, kk: lambda e: e.matmul(
                    ps_la[:, hh:hh + 1], w1s[:, hh % 2, kk, :],
                    Xq[:, kk, 512:513], start=(kk == 0),
                    stop=(kk == DCH - 1)))(h, k),
                    waits=lwts, inc="st" if k == DCH - 1 else None)
            w1mm_ev[h] = g["st"]
            mv = P.op("vector", (lambda bk, hh: lambda e: e.tensor_copy(
                hs[:, hh, 1:513], bk[:]))(bank, h),
                waits=[("st", w1mm_ev[h])], inc="sv")
            mev[h] = mv["sv"]
            s = P.op("scalar", (lambda hh: lambda e: e.activation(
                hs[:, hh, 513:514], ps_la[:, hh:hh + 1], AF.Copy))(h),
                waits=[("st", w1mm_ev[h])], inc="ss")
            hs_ev[h] = s["ss"]
            if h >= 2:
                emit_fsilu(h - 2)
            ca = fca[:, h % 2, :]
            cwts = []
            if h >= 2:
                cwts.append(("ss", hsil_ev[h - 2]))
            P.op("vector", (lambda hh, ca_: lambda e: e.tensor_scalar(
                ca_, hs[:, hh, 0:512],
                p_cw2[:, hh * 3:hh * 3 + 1], None, OP.mult))(h, ca),
                waits=cwts)
            P.op("vector", (lambda hh, ca_: lambda e:
                 e.scalar_tensor_tensor(
                     ca_, hs[:, hh, 1:513],
                     p_cw2[:, hh * 3 + 1:hh * 3 + 2], ca_,
                     OP.mult, OP.add))(h, ca))
            v = P.op("vector", (lambda hh, ca_: lambda e:
                 e.scalar_tensor_tensor(
                     ca_, hs[:, hh, 2:514],
                     p_cw2[:, hh * 3 + 2:hh * 3 + 3], ca_,
                     OP.mult, OP.add))(h, ca),
                waits=[("ss", hs_ev[h])], inc="sv")
            v3f_ev[h] = v["sv"]
        emit_fsilu(HCH - 2)
        emit_fsilu(HCH - 1)
        ev["la_prev"] = hs_ev[HCH - 1]
        if j + 1 < NBF:
            emit_ffn_load(b, j + 1)
        if debug and b == 0 and j == 0:
            P.op("sync", dma(dbg_h["d_hs"].rearrange("(c p) t -> p c t", p=128),
                             hs[:]), waits=[("ss", hsil_ev[HCH - 1])],
                 inc="so")

        if debug and b == 0 and j == 0:
            P.op("sync", dma(dbg_h["d_hsil"].rearrange("(c p) t -> p c t",
                                                       p=128), hsil[:]),
                 waits=[("ss", hsil_ev[HCH - 1])], inc="so")
        W2 = TBF
        t2maxb = btm[:, 0, 0:W2]
        t2minb = btm[:, 1, 0:W2]
        t2max = r2f[:, 0:W2]
        P.op("vector", lambda e: e.tensor_tensor(t2maxb, hsil[:, 0, :],
                                                 hsil[:, 1, :], OP.max),
             waits=[("ss", hsil_ev[1])])
        P.op("vector", lambda e: e.tensor_tensor(t2minb, hsil[:, 0, :],
                                                 hsil[:, 1, :], OP.min))
        for c in range(2, HCH):
            P.op("vector", (lambda cc: lambda e: e.tensor_tensor(
                t2maxb, t2maxb, hsil[:, cc, :], OP.max))(c),
                waits=[("ss", hsil_ev[c])])
            P.op("vector", (lambda cc: lambda e: e.tensor_tensor(
                t2minb, t2minb, hsil[:, cc, :], OP.min))(c))
        P.op("vector", lambda e: e.tensor_scalar(t2minb, t2minb, -1.0, None,
                                                 OP.mult))
        P.op("vector", lambda e: e.tensor_tensor(t2maxb, t2maxb, t2minb,
                                                 OP.max),
             waits=[("ld", ev.get("stats_ld", 0))])
        v = P.op("vector", lambda e: e.tensor_copy(t2max, t2maxb), inc="sv")
        s2_ld = emit_quant_scales(t2max, W2, 512, 1, s2_b, e2_b, v["sv"])
        ev["stats_ld"] = s2_ld
        vlast = None
        for g4 in range(8):
            sl = slice(g4 * 4, g4 * 4 + 4)
            P.op("vector", (lambda sl_: lambda e: e.tensor_tensor(
                tqg[:, :, 0:W2], hsil[:, sl_, :], brd(s2_b, 4, W2), OP.mult))
                (sl), waits=[("ld", s2_ld)])
            P.op("vector", lambda e: e.tensor_scalar(
                qz16[:, :, 0:W2], tqg[:, :, 0:W2], MAGIC, None, OP.add))
            vlast = P.op("vector", (lambda sl_: lambda e: e.tensor_scalar(
                hsil[:, sl_, :], qz16[:, :, 0:W2], -MAGIC, None, OP.add))(sl),
                inc="sv" if g4 == 7 else None)
        hq_ev = vlast["sv"]

        w2v = w2q_h.rearrange("(c p) dd -> p c dd", p=128)
        mm_ev = {}
        for dch in range(DCH):
            for half in range(2):
                idx = dch * 2 + half
                wts = [("so", ev["wq_done"])]
                if idx >= 2:
                    wts.append(("st", mm_ev[idx - 2]))
                else:
                    wts.append(("st", w1mm_ev[HCH - 1]))
                w = P.op("sync", (lambda dd, hh: dma(
                    w2c[:, hh],
                    w2v[:, hh * 16:(hh + 1) * 16,
                        dd * 128:(dd + 1) * 128]))(dch, half),
                    waits=wts, inc="ld")
                mwts = [("ld", w["ld"])]
                if idx == 0:
                    mwts.append(("sv", hq_ev))
                for k in range(16):
                    hk = half * 16 + k
                    g = P.op("tensor",
                             (lambda dd, hh, kk, hkk: lambda e: e.matmul(
                                 psb[dd][:], w2c[:, hh, kk, :],
                                 hsil[:, hkk, :], start=(hkk == 0),
                                 stop=(hkk == HCH - 1)))(dch, half, k, hk),
                             waits=mwts if k == 0 else (),
                             inc="st" if k == 15 else None)
                mm_ev[idx] = g["st"]
        mm2_ev = mm_ev[2 * DCH - 1]
        ev["mm2_prev"] = mm2_ev

        if j + 1 < NBF:
            emit_ffn_trees(b, j + 1)

        vlast = None
        for dch in range(DCH):
            P.op("vector", (lambda dd: lambda e: e.tensor_tensor(
                r2[:, dd, :], psb[dd][:], e2_b[:, 0:W2],
                OP.mult))(dch),
                waits=[("st", mm_ev[dch * 2 + 1])])
            vlast = P.op("vector", (lambda dd: lambda e: e.tensor_tensor(
                r2[:, dd, :], r2[:, dd, :], x1s[:, dd, 0:512], OP.add))(dch),
                inc="sv" if dch == DCH - 1 else None)
        r2_ev = vlast["sv"]
        ev[f"x1f_free{j % 2}"] = r2_ev

        if debug and b == 0 and j == 0:
            P.op("sync", dma(dbg_h["d_r2"].rearrange("(c p) t -> p c t", p=128),
                             r2[:]), waits=[("sv", r2_ev)], inc="so")
        s = P.op("scalar", lambda e: e.activation(hsil[:, 0:8, :], r2[:],
                                                  AF.Square),
                 waits=[("sv", r2_ev)], inc="ss")
        sq2_ev = s["ss"]
        v = P.op("vector", lambda e: e.tensor_copy(hs[:, 0:8, 0:512], r2[:]),
                 waits=[("sv", r2_ev)], inc="sv")
        r2bf_ev = v["sv"]
        for k in range(DCH):
            P.op("tensor", (lambda kk: lambda e: e.matmul(
                psb[4][:], ones128[:], hs[:, kk, 0:512],
                start=(kk == 0), stop=(kk == DCH - 1)))(k),
                waits=[("sv", r2bf_ev)] if k == 0 else ())
        for k in range(DCH):
            g = P.op("tensor", (lambda kk: lambda e: e.matmul(
                psb[5][:], ones128[:], hsil[:, kk, :],
                start=(kk == 0), stop=(kk == DCH - 1)))(k),
                waits=[("ss", sq2_ev)] if k == 0 else (),
                inc="st" if k == DCH - 1 else None)
        P.op("vector", lambda e: e.tensor_scalar(l2m[:], psb[4][:],
                                                 1.0 / D, None, OP.mult),
             waits=[("st", g["st"])])
        P.op("vector", lambda e: e.tensor_scalar(l2v[:], psb[5][:],
                                                 1.0 / D, None, OP.mult))
        P.op("vector", lambda e: e.tensor_tensor(l2r[:], l2m[:], l2m[:],
                                                 OP.mult))
        v = P.op("vector", lambda e: e.tensor_tensor(
            l2v[:], l2v[:], l2r[:], OP.subtract), inc="sv")
        P.op("scalar", lambda e: e.activation(l2r[:], l2v[:], AF.Ln,
                                              bias=1e-5),
             waits=[("sv", v["sv"])])
        s = P.op("scalar", lambda e: e.activation(l2r[:], l2r[:],
                                                  AF.Exp, scale=-0.5),
                 inc="ss")
        P.op("vector", lambda e: e.tensor_tensor(
            l2mr[:], l2m[:], l2r[:], OP.mult), waits=[("ss", s["ss"])])
        P.op("vector", lambda e: e.tensor_tensor(
            x2o[:], r2[:], brd(l2r, DCH, W2), OP.mult))
        v = P.op("vector", lambda e: e.tensor_tensor(
            x2o[:], x2o[:], brd(l2mr, DCH, W2), OP.subtract), inc="sv")
        st = P.op("sync", dma(out_h[b].rearrange("(c p) t -> p c t", p=128)
                              [:, :, t0:t0 + TBF], x2o[:]),
                  waits=[("sv", v["sv"])], inc="so")
        ev["blk_free"] = {"sv": v["sv"], "so": st["so"]}

    for b in range(BPC):
        for j in range(NBA):
            emit_attn_block(b, j)
        emit_ffn_load(b, 0)
        emit_ffn_trees(b, 0)
        for j in range(NBF):
            emit_ffn_block(b, j)

    fin = P.cur("so")
    P.ops.append(("sync", [("so", fin)], lambda e: None, []))

    with nc.Block() as block:
        for name in ("sync", "vector", "scalar", "tensor", "gpsimd"):
            def make(nm):
                def body(eng):
                    P.replay(nm, eng)
                return body
            getattr(block, name)(make(name))

    ctx.close()
    return nc


def _host_layout(inputs):
    f = np.ascontiguousarray
    i = inputs
    shared = {
        "winT": f(i["L0_attn_in_proj_w"].T),
        "wxpT": f(i["L0_attn_x_proj_w"].T),
        "wdtT": f(i["L0_attn_dt_proj_w"].T),
        "woutT": f(i["L0_attn_out_proj_w"].T),
        "wskT": f(i["L0_attn_skip_proj_w"].T),
        "w1T": f(i["L0_ffn_w1"].T),
        "w2T": f(i["L0_ffn_w2"].T),
        "b_in": f(i["L0_attn_in_proj_b"].reshape(16, 128).T),
        "cw1": f(i["L0_attn_conv_w"][:, 0, :].reshape(8, 128, 4)
                 .transpose(1, 0, 2).reshape(128, 32)),
        "cb1": f(i["L0_attn_conv_b"].reshape(8, 128).T),
        "bxp": f(i["L0_attn_x_proj_b"][:, None]),
        "bdt": f(i["L0_attn_dt_proj_b"].reshape(8, 128).T),
        "alog": f(np.repeat(i["L0_attn_A_log"].reshape(1, 16), 128, 0)),
        "dpar": f(i["L0_attn_D_param"].reshape(8, 128).T),
        "b_out": f(i["L0_attn_out_proj_b"].reshape(8, 128).T),
        "b_sk": f(i["L0_attn_skip_proj_b"].reshape(8, 128).T),
        "cw2": f(i["L0_ffn_conv_w"][:, 0, :].reshape(32, 128, 3)
                 .transpose(1, 0, 2).reshape(128, 96)),
        "cb2": f(i["L0_ffn_conv_b"].reshape(32, 128).T),
        "ident": np.eye(128, dtype=np.float32),
    }
    shared = {k: np.asarray(v, np.float32) for k, v in shared.items()}
    x = i["L0_attn_x"]
    in_maps = []
    for c in range(NC):
        m = dict(shared)
        m["xT"] = np.asarray(
            f(x[c * BPC:(c + 1) * BPC].transpose(0, 2, 1)), np.float32)
        in_maps.append(m)
    return in_maps


def kernel(**inputs):
    if "nc" not in _CACHE:
        _CACHE["nc"] = build_program(debug=False)
    nc = _CACHE["nc"]
    in_maps = _host_layout(inputs)
    res = run_bass_kernel_spmd(nc, in_maps, core_ids=list(range(NC)))
    outs = []
    for c in range(NC):
        o = np.asarray(res.results[c]["out"])
        outs.append(o.transpose(0, 2, 1))
    return np.ascontiguousarray(np.concatenate(outs, axis=0))
